# version 39
# baseline (speedup 1.0000x reference)
import os
import numpy as np
import ml_dtypes
from contextlib import ExitStack

import concourse.bass as bass
import concourse.mybir as mybir
from concourse.bass_utils import run_bass_kernel_spmd

F32 = mybir.dt.float32
BF16 = mybir.dt.bfloat16
AF = mybir.ActivationFunctionType
ALU = mybir.AluOpType

T = 2048
C = 1024
H = 16
HS = 64
NT = T // 128
NC = C // 128
NQC = T // 512
EPS = 1e-5


class Sem:
    def __init__(self, ctx, nc, name):
        self.h = ctx.enter_context(nc.semaphore(name))
        self.n = 0


class P:

    def __init__(self, target, eng, sems):
        self.target = target
        self.eng = eng
        self.sems = sems
        for s in sems.values():
            s.n = 0
        self._waited = {}

    def do(self, engine, fn, inc=None):
        if engine == self.target:
            inst = fn(self.eng)
            if inc is not None:
                for s, k in inc:
                    inst.then_inc(s.h, k)
        if inc is not None:
            for s, k in inc:
                s.n += k

    def wait(self, engine, sem, val=None):
        v = sem.n if val is None else val
        if v <= 0:
            return
        if engine == self.target:
            key = id(sem)
            if self._waited.get(key, -1) >= v:
                return
            self._waited[key] = v
            self.eng.wait_ge(sem.h, v)


def _carve(ar, byte_off, dtype, shape):
    esz = 2 if dtype == BF16 else 4
    n = 1
    for d in shape:
        n *= d
    v = ar[:, byte_off // 2: byte_off // 2 + (n * esz) // 2]
    if dtype != BF16:
        v = v.bitcast(dtype)
    if len(shape) == 1:
        return v
    spec = "p (" + " ".join(f"d{i}" for i in range(len(shape))) + ") -> p " + \
           " ".join(f"d{i}" for i in range(len(shape)))
    kw = {f"d{i}": shape[i] for i in range(len(shape))}
    return v.rearrange(spec, **kw)


def build_program(debug=False, upto=99):
    nc = bass.Bass("TRN2", target_bir_lowering=False)
    ctx = ExitStack()

    di = lambda n, s, d: nc.dram_tensor(n, s, d, kind="ExternalInput").ap()
    x_d = di("x", [T, C], F32)
    wqkv_d = di("wqkv", [H, 3 * HS, C], F32)
    wproj_d = di("wproj", [C, C], F32)
    bproj_d = di("bproj", [1, C], F32)
    w1_d = di("w1", [C, C], F32)
    w2_d = di("w2", [C, C], F32)
    w3_d = di("w3", [C, C], F32)
    fbias_d = di("fbias", [1, C], F32)
    out_d = nc.dram_tensor("out", [T, C], F32, kind="ExternalOutput").ap()

    wqkv_bf_d = nc.dram_tensor("wqkv_bf", [3 * C, C], BF16, kind="Internal").ap()
    wproj_bf_d = nc.dram_tensor("wproj_bf", [C, C], BF16, kind="Internal").ap()
    dbg = {}
    if debug:
        dd = lambda n, s_, d: nc.dram_tensor(n, s_, d, kind="ExternalOutput").ap()
        dbg["x1T"] = dd("dbg_x1T", [128, NC, T], BF16)
        dbg["v65"] = dd("dbg_v65", [128, NT * H * 65], BF16)
        dbg["oT"] = dd("dbg_oT", [128, NC, T], BF16)
        dbg["x2a"] = dd("dbg_x2a", [128, NT // 2, C], F32)
        dbg["x2b"] = dd("dbg_x2b", [128, NT // 2, C], F32)
        dbg["st4"] = dd("dbg_st4", [128, 2, NT], F32)
        dbg["fbf"] = dd("dbg_fbf", [128, NT, C], BF16)

    x_t = x_d.rearrange("(t p) c -> t p c", p=128)
    out_t = out_d.rearrange("(t p) c -> t p c", p=128)

    sbr = lambda n, s, d: ctx.enter_context(nc.sbuf_tensor(n, s, d, side="right"))
    ident = sbr("ident", [128, 128], BF16)
    trineg = sbr("trineg", [128, 128], F32)
    ones1 = sbr("ones1", [1, 128], BF16)
    bprow_bf = sbr("bprow_bf", [1, C], BF16)
    fbrow_bf = sbr("fbrow_bf", [1, C], BF16)
    bias_bc = sbr("bias_bc", [128, C], F32)
    stA = [sbr(f"stA{i}", [128, 12], F32) for i in range(2)]
    stB = [sbr(f"stB{i}", [128, 2], F32) for i in range(2)]
    sd_t = [sbr(f"sd{i}", [128, 1], F32) for i in range(2)]
    rin_t = [sbr(f"rin{i}", [128, 1], F32) for i in range(2)]
    stB_all = sbr("stB_all", [128, 2, NT], F32)
    sd_all = sbr("sd_all", [128, NT], F32)
    rin_all = sbr("rin_all", [128, NT], F32)
    lnl_bf = [sbr(f"lnl{i}", [1, 512], BF16) for i in range(2)]
    lbc_sb = [sbr(f"lbc{i}", [64, 512], BF16) for i in range(2)]
    oTun = [sbr(f"oTun{i}", [65, 512], BF16) for i in range(2)]
    eps_t = sbr("eps_t", [128, 1], F32)
    qscale_t = sbr("qscale_t", [128, 1], F32)

    KB = 1024
    arA = ctx.enter_context(nc.sbuf_tensor("arA", [128, 28 * KB], BF16)).ap()
    arB = ctx.enter_context(nc.sbuf_tensor("arB", [128, 16 * KB], BF16)).ap()
    arC = ctx.enter_context(nc.sbuf_tensor("arC", [128, 16 * KB + 256], BF16)).ap()
    arD = ctx.enter_context(nc.sbuf_tensor("arD", [128, 16 * KB], BF16)).ap()
    arE = ctx.enter_context(nc.sbuf_tensor("arE", [128, 16 * KB], BF16)).ap()

    wqkvT = _carve(arA, 0, BF16, [NC, 3 * C])
    xa = [_carve(arD, i * 16 * KB, F32, [4, C]) for i in range(2)]
    x1bf = [_carve(arE, i * 2 * KB, BF16, [C]) for i in range(2)]
    pTt = [_carve(arA, 48 * KB + i * 2 * KB, BF16, [1024]) for i in range(2)]
    x3bf = [_carve(arA, 48 * KB + i * 2 * KB, BF16, [C]) for i in range(2)]
    wprojT = _carve(arA, 0, BF16, [NC, C])
    W1sb = _carve(arA, 16 * KB, BF16, [NC, C])
    W2sb = _carve(arA, 32 * KB, BF16, [NC, C])
    W3sb = _carve(arA, 0, BF16, [NC, C])
    xa2 = [_carve(arA, 48 * KB + i * 4 * KB, F32, [C]) for i in range(2)]
    h3sb = [_carve(arA, 16 * KB + i * 4 * KB, F32, [C]) for i in range(2)]
    fsb = [_carve(arA, 24 * KB + i * 4 * KB, F32, [C]) for i in range(2)]
    lnf = [_carve(arA, 48 * KB + i * 4 * KB, F32, [C]) for i in range(2)]
    outt = [_carve(arA, 32 * KB + i * 8 * KB, F32, [2, C]) for i in range(2)]

    x1T = _carve(arB, 0, BF16, [NC, T])
    x2b = _carve(arB, 0, F32, [NT // 2, C])

    v65 = _carve(arC, 0, BF16, [NT, H * 65])
    o2sb = [_carve(arC, i * 4 * KB, F32, [C]) for i in range(2)]
    lnout = [_carve(arC, 8 * KB + i * 4 * KB, F32, [C]) for i in range(2)]
    AB1 = _carve(arC, 0, BF16, [NC, T])
    f_bf = _carve(arC, 0, BF16, [NT, C])

    qkT = [_carve(arD, j * 8 * KB, BF16, [2, T]) for j in range(2)]
    x2a = _carve(arD, 0, F32, [NT // 2, C])
    oT = _carve(arE, 0, BF16, [NC, T])
    AB0 = _carve(arE, 0, BF16, [NC, T])

    def x2v(t):
        return x2a[:, t, :] if t < NT // 2 else x2b[:, t - NT // 2, :]

    v65v = v65.rearrange("p t (h e) -> p t h e", h=H)
    wqkvT3 = wqkvT.rearrange("p c (w r) -> p c w r", w=3)

    sems = {}
    for name in ["sLx0", "sLx1", "sLW", "sLW2", "sLP", "sG", "sGB", "sGQ", "sGQ2",
                 "sGPc", "sGW", "sP", "sV", "sS", "sO0", "sO1", "sDG"]:
        sems[name] = Sem(ctx, nc, name)

    guards = {}
    alloc_cache = {}
    walk_state = {"first": True}

    def ps_open(name, shape, dtype):
        if walk_state["first"]:
            g = nc.psum_tensor(name, shape, dtype)
            tt = g.__enter__()
            guards[name] = g
            alloc_cache[name] = tt
        return alloc_cache[name]

    def ps_close(*names):
        if walk_state["first"]:
            for name in names:
                guards.pop(name).__exit__(None, None, None)

    blk = ctx.enter_context(nc.Block())

    def walk(p: P):
        def dump(key, src_ap, wait_sem=None, wait_val=None, after_inc=None):
            if not debug:
                return
            if wait_sem is not None:
                p.wait("gpsimd", wait_sem, wait_val)
            p.do("gpsimd", lambda g: g.dma_start(out=dbg[key], in_=src_ap),
                 inc=[(p.sems["sDG"], 16)])
            p.wait("gpsimd", p.sems["sDG"])
            if after_inc is not None:
                p.do("gpsimd", lambda g: g.engine_nop(), inc=[(after_inc, 1)])
        (sLx0, sLx1, sLW, sLW2, sLP, sG, sGB, sGQ, sGQ2, sGPc, sGW, sP, sV, sS,
         sO0, sO1) = (
            p.sems[k] for k in
            ["sLx0", "sLx1", "sLW", "sLW2", "sLP", "sG", "sGB", "sGQ", "sGQ2",
             "sGPc", "sGW", "sP", "sV", "sS", "sO0", "sO1"])
        sLx = [sLx0, sLx1]
        sO = [sO0, sO1]

        p.do("gpsimd", lambda g: g.memset(ident[:, :], 0.0))
        p.do("gpsimd", lambda g: g.memset(trineg[:, :], 0.0))
        p.do("gpsimd", lambda g: g.memset(ones1[:, :], 1.0))
        p.do("gpsimd", lambda g: g.memset(eps_t[:, :], EPS))
        p.do("gpsimd", lambda g: g.memset(qscale_t[:, :], 0.125), inc=[(sG, 1)])
        p.wait("gpsimd", sG)
        p.do("gpsimd", lambda g: g.affine_select(
            out=ident[:, :], in_=ident[:, :], compare_op=ALU.not_equal, fill=1.0,
            base=0, pattern=[[-1, 128]], channel_multiplier=1))
        p.do("gpsimd", lambda g: g.affine_select(
            out=trineg[:, :], in_=trineg[:, :], compare_op=ALU.is_ge, fill=-1e9,
            base=0, pattern=[[1, 128]], channel_multiplier=-1), inc=[(sG, 1)])
        const_done = sG.n
        p.do("gpsimd", lambda g: g.dma_start(out=bprow_bf[:, :], in_=bproj_d),
             inc=[(sGB, 16)])
        p.do("gpsimd", lambda g: g.dma_start(out=fbrow_bf[:, :], in_=fbias_d),
             inc=[(sGB, 16)])
        bias_bf_done = sGB.n

        p.wait("scalar", sG, const_done)
        p.wait("vector", sG, const_done)

        pbias = ps_open("pbias", [128, C], F32)
        p.wait("tensor", sGB, bias_bf_done)
        p.do("tensor", lambda t: t.matmul(pbias[:, 0:512], ones1[:, :],
                                          bprow_bf[:, 0:512], start=True, stop=True))
        p.do("tensor", lambda t: t.matmul(pbias[:, 512:1024], ones1[:, :],
                                          bprow_bf[:, 512:1024], start=True, stop=True),
             inc=[(sP, 1)])
        p.wait("vector", sP)
        p.do("vector", lambda v: v.tensor_copy(bias_bc[:, :], pbias[:, :]), inc=[(sV, 1)])
        bpbc_done = sV.n
        ps_close("pbias")

        if upto < 10:
            return
        ptr = [ps_open(f"ptr{i}", [128, C], BF16) for i in range(2)]
        xa_free = [None] * 2
        x1bf_free = [None] * 2
        ptr_free = [None] * 2
        stats_free = [None] * 2
        wqkvT_done = 0
        def p1_load2(tp4):
            b = tp4 % 2
            if xa_free[b] is not None:
                p.wait("sync", *xa_free[b])
            src = x_d[tp4 * 512:(tp4 + 1) * 512, :].rearrange("(u p) c -> p u c", p=128)
            nsp = 4
            uw = 4 // nsp
            for i in range(nsp):
                p.do("sync", lambda s, b=b, src=src, i=i, uw=uw: s.dma_start(
                    out=xa[b][:, i * uw:(i + 1) * uw, :],
                    in_=src[:, i * uw:(i + 1) * uw, :]), inc=[(sLx[b], 16)])
            return (sLx[b], sLx[b].n)

        def p1_stats(t, x_loaded):
            sb = t % 2
            u = t % 4
            b = (t // 4) % 2
            p.wait("vector", *x_loaded)
            if stats_free[sb] is not None:
                p.wait("vector", *stats_free[sb])
            p.do("vector", lambda v, b=b, u=u, sb=sb: v.bn_stats(stA[sb][:, 0:6], xa[b][:, u, 0:512]))
            p.do("vector", lambda v, b=b, u=u, sb=sb: v.bn_stats(stA[sb][:, 6:12], xa[b][:, u, 512:1024]),
                 inc=[(sV, 1)])
            p.wait("vector", sV)
            p.do("vector", lambda v, sb=sb: v.bn_aggr(stB[sb][:, :], stA[sb][:, :]), inc=[(sV, 1)])
            aggr_done = sV.n
            p.wait("scalar", sV, aggr_done)
            p.do("scalar", lambda s, sb=sb: s.activation(sd_t[sb][:, :], stB[sb][:, 1:2],
                                                         AF.Sqrt, bias=eps_t[:, :]), inc=[(sS, 1)])
            return sS.n

        def p1_apply(t, sq_done):
            sb = t % 2
            u = t % 4
            b = (t // 4) % 2
            p.wait("vector", sS, sq_done)
            p.do("vector", lambda v, sb=sb: v.reciprocal(rin_t[sb][:, :], sd_t[sb][:, :]),
                 inc=[(sV, 1)])
            p.wait("vector", sV)
            if x1bf_free[sb] is not None:
                p.wait("vector", *x1bf_free[sb])
            p.do("vector", lambda v, b=b, u=u, sb=sb: v.tensor_scalar(
                x1bf[sb][:, :], xa[b][:, u, :], stB[sb][:, 0:1], rin_t[sb][:, :],
                ALU.subtract, ALU.mult), inc=[(sV, 1)])
            apply_done = sV.n
            if t % 4 == 3:
                xa_free[b] = (sV, apply_done)
            stats_free[sb] = (sV, apply_done)
            p.wait("tensor", sV, apply_done)
            p.wait("tensor", sG, const_done)
            if ptr_free[sb] is not None:
                p.wait("tensor", *ptr_free[sb])
            for cc in range(NC):
                p.do("tensor", lambda te, sb=sb, cc=cc: te.transpose(
                    ptr[sb][:, cc * 128:(cc + 1) * 128],
                    x1bf[sb][:, cc * 128:(cc + 1) * 128], ident[:, :]),
                    inc=[(sP, 1)] if cc == NC - 1 else None)
            tr_done = sP.n
            x1bf_free[sb] = (sP, tr_done)
            p.wait("scalar", sP, tr_done)
            p.do("scalar", lambda s, sb=sb, t=t: s.copy(
                x1T[:, :, t * 128:(t + 1) * 128],
                ptr[sb].rearrange("p (c q) -> p c q", c=NC)), inc=[(sS, 1)])
            ptr_free[sb] = (sS, sS.n)

        ld_rec = {}
        sq_rec = {}
        rec = {}
        ld_rec[0] = p1_load2(0)
        ld_rec[1] = p1_load2(1)
        sq_rec[0] = p1_stats(0, ld_rec[0])
        for hg in range(4):
            p.do("gpsimd", lambda g, hg=hg: g.dma_start(
                out=wqkv_bf_d[2 * C + hg * 256:2 * C + (hg + 1) * 256, :]
                    .rearrange("(h d) c -> h d c", h=4),
                in_=wqkv_d[hg * 4:(hg + 1) * 4, 2 * HS:3 * HS, :]), inc=[(sGQ2, 16)])
        wqkv2_cast_done = sGQ2.n
        p.do("gpsimd", lambda g: g.memset(v65v[:, :, :, 64:65], 1.0), inc=[(sG, 1)])
        v65ones_done = sG.n

        p.wait("sync", sGQ2, wqkv2_cast_done)
        for cc in range(NC):
            p.do("sync", lambda s, cc=cc: s.dma_start_transpose(
                out=wqkvT3[:, cc, 2, :],
                in_=wqkv_bf_d[2 * C:3 * C, cc * 128:(cc + 1) * 128]),
                inc=[(sLW2, 16)])
        rec["wqkvT2_done"] = sLW2.n

        p.wait("gpsimd", sLW2, rec["wqkvT2_done"])
        for w in (0, 1):
            for hh in range(2):
                p.do("gpsimd", lambda g, w=w, hh=hh: g.dma_start(
                    out=wqkv_bf_d[w * C + hh * 512:w * C + (hh + 1) * 512, :]
                        .rearrange("(h d) c -> h d c", h=8),
                    in_=wqkv_d[hh * 8:(hh + 1) * 8, w * HS:(w + 1) * HS, :]),
                    inc=[(sGQ, 16)])
        wqkv_cast_done = sGQ.n

        pv = [ps_open(f"pv{i}", [128, C], F32) for i in range(2)]
        pv_free = [None] * 2
        x1T_tile = {}
        VSKEW = 6

        def emit_v(t):
            pb = t % 2
            if pv_free[pb] is not None:
                p.wait("tensor", *pv_free[pb])
            p.wait("tensor", sS, x1T_tile[t])
            p.wait("tensor", sLW2, rec["wqkvT2_done"])
            for half in range(2):
                for cc in range(NC):
                    p.do("tensor", lambda te, t=t, pb=pb, half=half, cc=cc: te.matmul(
                        pv[pb][:, half * 512:(half + 1) * 512],
                        x1T[:, cc, t * 128:(t + 1) * 128],
                        wqkvT3[:, cc, 2, half * 512:(half + 1) * 512],
                        start=(cc == 0), stop=(cc == NC - 1)),
                        inc=[(sP, 1)] if (cc == NC - 1 and half == 1) else None)
            v_mm_done = sP.n
            p.wait("scalar", sP, v_mm_done)
            p.do("scalar", lambda s, t=t, pb=pb: s.copy(
                v65v[:, t, :, 0:64],
                pv[pb].rearrange("p (h e) -> p h e", h=H)), inc=[(sS, 1)])
            pv_free[pb] = (sS, sS.n)

        for t in range(NT):
            g = t // 4 + 1
            if t % 4 == 0 and 2 <= g < NT // 4:
                ld_rec[g] = p1_load2(g)
            if t + 1 < NT:
                sq_rec[t + 1] = p1_stats(t + 1, ld_rec[(t + 1) // 4])
            p1_apply(t, sq_rec[t])
            x1T_tile[t] = sS.n
            if t >= VSKEW:
                emit_v(t - VSKEW)
        for t in range(NT - VSKEW, NT):
            emit_v(t)
        x1T_done_S = x1T_tile[NT - 1]
        v_done_S = sS.n

        p.wait("sync", sGQ, wqkv_cast_done)
        for w in range(2):
            for cc in range(NC):
                p.do("sync", lambda s, w=w, cc=cc: s.dma_start_transpose(
                    out=wqkvT3[:, cc, w, :],
                    in_=wqkv_bf_d[w * C:(w + 1) * C, cc * 128:(cc + 1) * 128]),
                    inc=[(sLW, 16)])
        rec["wqkvT_done"] = sLW.n
        ps_close("pv1", "pv0", "ptr1", "ptr0")

        if upto < 20:
            return
        dump("x1T", x1T[:, :, :], sS, x1T_done_S)
        p.wait("gpsimd", sS, x1T_done_S)
        p.do("gpsimd", lambda g: g.dma_start(out=wproj_bf_d, in_=wproj_d),
             inc=[(sGPc, 16)])
        wproj_cast_done = sGPc.n

        if upto < 30:
            return
        pqk = [ps_open(f"pqk{i}", [128, 512], F32) for i in range(2)]
        pqk_free = [None] * 2
        p.wait("tensor", sS, v_done_S)
        filler_done = {}
        qk_rec = {}
        rec2 = {}

        def qk_units(pnext):
            slot = pnext % 2
            for which in range(2):
                for tq in range(NQC):
                    alt = (which * NQC + tq) % 2
                    for cc in range(NC):
                        def mm(cc=cc, which=which, tq=tq, alt=alt, pnext=pnext):
                            if cc == 0 and pqk_free[alt] is not None:
                                p.wait("tensor", *pqk_free[alt])
                            p.do("tensor", lambda te, cc=cc, which=which, tq=tq, alt=alt, pnext=pnext: te.matmul(
                                pqk[alt][:, :],
                                wqkvT3[:, cc, which, pnext * 128:(pnext + 1) * 128],
                                x1T[:, cc, tq * 512:(tq + 1) * 512],
                                start=(cc == 0), stop=(cc == NC - 1)),
                                inc=[(sP, 1)] if cc == NC - 1 else None)
                            if cc == NC - 1:
                                qk_rec[(pnext, which, tq)] = sP.n
                        yield mm

                    def evac(which=which, tq=tq, alt=alt, slot=slot, pnext=pnext):
                        p.wait("vector", sP, qk_rec[(pnext, which, tq)])
                        if which == 0:
                            p.do("vector", lambda v, alt=alt, slot=slot, tq=tq: v.tensor_scalar_mul(
                                qkT[slot][:, 0, tq * 512:(tq + 1) * 512],
                                pqk[alt][:, :], qscale_t[:, :]), inc=[(sV, 1)])
                        else:
                            p.do("vector", lambda v, alt=alt, slot=slot, tq=tq: v.tensor_copy(
                                qkT[slot][:, 1, tq * 512:(tq + 1) * 512],
                                pqk[alt][:, :]), inc=[(sV, 1)])
                        pqk_free[alt] = (sV, sV.n)
                    yield evac

        p.wait("tensor", sLW, rec["wqkvT_done"])
        for u in qk_units(0):
            u()
        filler_done[0] = (sV, sV.n)

        NPAR = 2
        ps_s = [ps_open(f"pss{i}", [128, 1024], F32) for i in range(NPAR)]
        ps_o = [ps_open(f"po{h}", [128, 512], F32) for h in range(2)]
        pT = pTt
        ps_s_free = [None] * NPAR
        pT_free = [None] * NPAR
        ps_o_free = [None, None]
        lbc_free = [None, None]
        lnl_free = [None, None]
        oTun_free = [None, None]
        tail_stash = [None]

        p.wait("tensor", sG, v65ones_done)

        tail_q = []
        ucnt = [0]
        cur_fil = [iter(())]

        def bcast_unit(h, lnl_val, pp, qc, alt):
            p.wait("tensor", sS, lnl_val)
            if lbc_free[h] is not None:
                p.wait("tensor", *lbc_free[h])
            if pqk_free[alt] is not None:
                p.wait("tensor", *pqk_free[alt])
            p.do("tensor", lambda te, h=h, alt=alt: te.matmul(
                pqk[alt][0:64, :], ones1[:, 0:64], lnl_bf[h][:, :],
                start=True, stop=True), inc=[(sP, 1)])
            lbc_mm = sP.n
            lnl_free[h] = (sP, lbc_mm)
            p.wait("scalar", sP, lbc_mm)
            p.do("scalar", lambda s, h=h, alt=alt: s.activation(
                lbc_sb[h][:, :], pqk[alt][0:64, :], AF.Exp, scale=-1.0), inc=[(sS, 1)])
            lbcexp_done = sS.n
            pqk_free[alt] = (sS, lbcexp_done)
            lo = h * 64
            p.wait("vector", sS, lbcexp_done)
            p.do("vector", lambda v, h=h, pp=pp, qc=qc, lo=lo: v.tensor_tensor(
                oT[lo:lo + 64, pp, qc * 512:(qc + 1) * 512],
                oTun[h][0:64, :], lbc_sb[h][:, :], ALU.mult), inc=[(sV, 1)])
            done = sV.n
            oTun_free[h] = (sV, done)
            lbc_free[h] = (sV, done)

        popped_grp = [-1]

        def pop_bcast():
            tail_q.pop(0)(((ucnt[0] // 9) + 1) % 2)
            popped_grp[0] = ucnt[0] // 9

        def flush_tail_q():
            while tail_q:
                if ucnt[0] % 9 == 0 and popped_grp[0] != ucnt[0] // 9:
                    pop_bcast()
                    continue
                u = next(cur_fil[0], None)
                if u is None:
                    tail_q.pop(0)(((ucnt[0] // 9) + 1) % 2)
                else:
                    u()
                    ucnt[0] += 1

        def emit_tail():
            if tail_stash[0] is None:
                return
            flush_tail_q()
            pp, qc, ov_get = tail_stash[0]
            ov_done = ov_get()
            tail_stash[0] = None
            for h in range(2):
                po = ps_o[h]
                p.wait("vector", sP, ov_done)
                if oTun_free[h] is not None:
                    p.wait("vector", *oTun_free[h])
                p.do("vector", lambda v, h=h, po=po: v.tensor_copy(oTun[h][:, :], po[0:65, :]),
                     inc=[(sV, 1)])
                otun_done = sV.n
                ps_o_free[h] = (sV, otun_done)
                p.wait("scalar", sV, otun_done)
                if lnl_free[h] is not None:
                    p.wait("scalar", *lnl_free[h])
                p.do("scalar", lambda s, h=h: s.activation(
                    lnl_bf[h][:, :], oTun[h][64:65, :], AF.Ln), inc=[(sS, 1)])
                lnl_done = sS.n
                tail_q.append(lambda alt, h=h, lnl_done=lnl_done, pp=pp, qc=qc: bcast_unit(
                    h, lnl_done, pp, qc, alt))

        for pp in range(8):
            slot = pp % 2
            cur_fil[0] = qk_units(pp + 1) if pp < 7 else iter(())

            def fill(n):
                for _ in range(n):
                    if tail_q and popped_grp[0] != ucnt[0] // 9:
                        pop_bcast()
                    u = next(cur_fil[0], None)
                    if u is None:
                        return
                    u()
                    ucnt[0] += 1

            p.wait("tensor", *filler_done[pp])
            exp_rec = {}
            pend = []
            last_av_P = {}

            def emit_av(ent, pp=pp, exp_rec=exp_rec, last_av_P=last_av_P):
                qc_, nkc_, kc = ent
                par = kc % NPAR
                sem, val = exp_rec[(qc_, kc)]
                p.wait("tensor", sem, val)
                for h in range(2):
                    if kc == 0 and ps_o_free[h] is not None:
                        p.wait("tensor", *ps_o_free[h])
                    p.do("tensor", lambda te, pp=pp, kc=kc, h=h, par=par, nkc_=nkc_: te.matmul(
                        ps_o[h][0:65, :],
                        v65v[:, kc, 2 * pp + h, :],
                        pT[par][:, h * 512:(h + 1) * 512],
                        start=(kc == 0), stop=(kc == nkc_ - 1)), inc=[(sP, 1)])
                pT_free[par] = (sP, sP.n)
                last_av_P[qc_] = sP.n

            for qc in range(NQC):
                nkc = 4 * qc + 4

                for kc in range(nkc):
                    par = kc % NPAR
                    diag = kc >= 4 * qc
                    dq = 128 * (kc - 4 * qc) if diag else 0
                    if ps_s_free[par] is not None:
                        p.wait("tensor", *ps_s_free[par])
                    for h in range(2):
                        lo = h * 64
                        p.do("tensor", lambda te, pp=pp, qc=qc, kc=kc, h=h, par=par, lo=lo, slot=slot: te.matmul(
                            ps_s[par][:, h * 512:(h + 1) * 512],
                            qkT[slot][lo:lo + 64, 1, kc * 128:(kc + 1) * 128],
                            qkT[slot][lo:lo + 64, 0, qc * 512:(qc + 1) * 512],
                            start=True, stop=True),
                            inc=[(sP, 1)] if h == 1 else None)
                    s_done = sP.n
                    if len(pend) > 1:
                        emit_av(pend.pop(0))
                    if kc == 1:
                        emit_tail()
                    fill(2)
                    if diag:
                        p.wait("vector", sP, s_done)
                        for h in range(2):
                            p.do("vector", lambda v, h=h, par=par, dq=dq: v.tensor_tensor(
                                ps_s[par][:, h * 512 + dq:h * 512 + dq + 128],
                                ps_s[par][:, h * 512 + dq:h * 512 + dq + 128],
                                trineg[:, :], ALU.add),
                                inc=[(sV, 1)] if h == 1 else None)
                        smask_done = sV.n
                    if diag:
                        p.wait("scalar", sV, smask_done)
                    else:
                        p.wait("scalar", sP, s_done)
                    if pT_free[par] is not None:
                        p.wait("scalar", *pT_free[par])
                    p.do("scalar", lambda s, par=par, dq=dq: s.activation(
                        pT[par][:, dq:1024], ps_s[par][:, dq:1024], AF.Exp), inc=[(sS, 1)])
                    ps_s_free[par] = (sS, sS.n)
                    gate = (sS, sS.n)
                    if dq > 0:
                        p.wait("gpsimd", sS, gate[1])
                        p.do("gpsimd", lambda g, par=par, dq=dq: g.memset(
                            pT[par][:, 0:dq], 0.0))
                        p.do("gpsimd", lambda g, par=par, dq=dq: g.memset(
                            pT[par][:, 512:512 + dq], 0.0), inc=[(sG, 1)])
                        gate = (sG, sG.n)
                    exp_rec[(qc, kc)] = gate
                    pend.append((qc, nkc, kc))
                    fill(1)
                tail_stash[0] = (pp, qc, (lambda d=last_av_P, q=qc: d[q]))
                if qc == 2:
                    fill(10 ** 6)
            fill(10 ** 6)
            for ent in pend:
                emit_av(ent)
            pend.clear()
            if pp < 7:
                filler_done[pp + 1] = (sV, sV.n)
            if pp == 6:
                rec2["fil7_P"] = sP.n
        emit_tail()
        flush_tail_q()
        attn_done_V = sV.n
        ps_close("po1", "po0", "pss1", "pss0", "pqk1", "pqk0")

        p.wait("sync", sGPc, wproj_cast_done)
        p.wait("sync", sP, rec2["fil7_P"])
        for cc in range(NC):
            p.do("sync", lambda s, cc=cc: s.dma_start_transpose(
                out=wprojT[:, cc, :], in_=wproj_bf_d[:, cc * 128:(cc + 1) * 128]),
                inc=[(sLP, 16)])
        wprojT_done = sLP.n
        p.wait("gpsimd", sP, rec2["fil7_P"])
        p.do("gpsimd", lambda g: g.dma_start(
            out=W1sb[:, :, :], in_=w1_d.rearrange("(cc p) f -> p cc f", p=128)),
            inc=[(sGW, 16)])
        p.do("gpsimd", lambda g: g.dma_start(
            out=W2sb[:, :, :], in_=w2_d.rearrange("(cc p) f -> p cc f", p=128)),
            inc=[(sGW, 16)])
        W1_done = sGW.n
        W2_done = sGW.n

        if upto < 50:
            return
        oT_dumped = Sem.__new__(Sem); oT_dumped = None
        if debug:
            dump("oT", oT[:, :, :], sV, attn_done_V, after_inc=sG)
            oT_dump_done = sG.n
        po2 = [ps_open(f"po2{i}", [128, C], F32) for i in range(2)]
        po2_free = [None] * 2
        o2_free = [None] * 2
        ln_free = [None] * 2
        stats_free = [None] * 2
        xa2_free = [None] * 2
        p.wait("sync", sV, attn_done_V)
        p.wait("tensor", sV, attn_done_V)
        p.wait("tensor", sLP, wprojT_done)
        p.wait("tensor", sV, bpbc_done)

        def p3_front(t):
            pb = t % 2
            if xa2_free[pb] is not None:
                p.wait("sync", *xa2_free[pb])
            p.do("sync", lambda s, t=t, pb=pb: s.dma_start(out=xa2[pb][:, :], in_=x_t[t]),
                 inc=[(sLx[pb], 16)])
            x2_loaded = (sLx[pb], sLx[pb].n)
            if po2_free[pb] is not None:
                p.wait("tensor", *po2_free[pb])
            for cc in range(NC):
                for fc in range(2):
                    p.do("tensor", lambda te, t=t, pb=pb, cc=cc, fc=fc: te.matmul(
                        po2[pb][:, fc * 512:(fc + 1) * 512],
                        oT[:, cc, t * 128:(t + 1) * 128],
                        wprojT[:, cc, fc * 512:(fc + 1) * 512],
                        start=(cc == 0), stop=(cc == NC - 1)),
                        inc=[(sP, 1)] if (cc == NC - 1 and fc == 1) else None)
            mm_done = sP.n
            p.wait("vector", sP, mm_done)
            if o2_free[pb] is not None:
                p.wait("vector", *o2_free[pb])
            p.do("vector", lambda v, pb=pb: v.tensor_tensor(
                o2sb[pb][:, :], po2[pb][:, :], bias_bc[:, :], ALU.add), inc=[(sV, 1)])
            po2_free[pb] = (sV, sV.n)
            p.wait("vector", sV)
            if stats_free[pb] is not None:
                p.wait("vector", *stats_free[pb])
            p.do("vector", lambda v, pb=pb: v.bn_stats(stA[pb][:, 0:6], o2sb[pb][:, 0:512]))
            p.do("vector", lambda v, pb=pb: v.bn_stats(stA[pb][:, 6:12], o2sb[pb][:, 512:1024]),
                 inc=[(sV, 1)])
            p.wait("vector", sV)
            p.do("vector", lambda v, pb=pb: v.bn_aggr(stB[pb][:, :], stA[pb][:, :]), inc=[(sV, 1)])
            aggr_done = sV.n
            p.wait("scalar", sV, aggr_done)
            p.do("scalar", lambda s, pb=pb: s.activation(sd_t[pb][:, :], stB[pb][:, 1:2],
                                                         AF.Sqrt, bias=eps_t[:, :]), inc=[(sS, 1)])
            return (sS.n, x2_loaded)

        def p3_back(t, rec):
            pb = t % 2
            sq_done, x2_loaded = rec
            p.wait("vector", sS, sq_done)
            p.do("vector", lambda v, pb=pb: v.reciprocal(rin_t[pb][:, :], sd_t[pb][:, :]),
                 inc=[(sV, 1)])
            p.wait("vector", sV)
            if ln_free[pb] is not None:
                p.wait("vector", *ln_free[pb])
            p.do("vector", lambda v, pb=pb: v.tensor_scalar(
                lnout[pb][:, :], o2sb[pb][:, :], stB[pb][:, 0:1], rin_t[pb][:, :],
                ALU.subtract, ALU.mult), inc=[(sV, 1)])
            o2_free[pb] = (sV, sV.n)
            stats_free[pb] = (sV, sV.n)
            lnout_done = sV.n
            p.wait("gpsimd", sV, lnout_done)
            p.wait("gpsimd", *x2_loaded)
            p.do("gpsimd", lambda g, pb=pb, t=t: g.tensor_tensor(
                x2v(t), lnout[pb][:, :], xa2[pb][:, :], ALU.add), inc=[(sG, 1)])
            ln_free[pb] = (sG, sG.n)
            xa2_free[pb] = (sG, sG.n)

        p3_rec = {0: p3_front(0)}
        for t in range(NT):
            if t + 1 < NT:
                p3_rec[t + 1] = p3_front(t + 1)
            p3_back(t, p3_rec[t])
        x2_done_V = sV.n
        x2_done_G = sG.n
        ps_close("po21", "po20")

        if upto < 60:
            return
        if debug:
            dump("x2a", x2a[:, :, :], sG, x2_done_G)
            dump("x2b", x2b[:, :, :], sG, x2_done_G)
        ptrb = [ps_open(f"ptrb{i}", [128, C], BF16) for i in range(2)]
        ptr_free = [None] * 2
        x3bf_free = [None] * 2
        stats_free = [None] * 2
        p.wait("vector", sV, x2_done_V)
        p.wait("vector", sG, x2_done_G)

        def ln3_front(t):
            sb = t % 2
            if stats_free[sb] is not None:
                p.wait("vector", *stats_free[sb])
            p.do("vector", lambda v, t=t, sb=sb: v.bn_stats(stA[sb][:, 0:6], x2v(t)[:, 0:512]))
            p.do("vector", lambda v, t=t, sb=sb: v.bn_stats(stA[sb][:, 6:12], x2v(t)[:, 512:1024]),
                 inc=[(sV, 1)])
            p.wait("vector", sV)
            p.do("vector", lambda v, sb=sb: v.bn_aggr(stB[sb][:, :], stA[sb][:, :]), inc=[(sV, 1)])
            aggr_done = sV.n
            p.wait("scalar", sV, aggr_done)
            p.do("scalar", lambda s, sb=sb: s.activation(sd_t[sb][:, :], stB[sb][:, 1:2],
                                                         AF.Sqrt, bias=eps_t[:, :]), inc=[(sS, 1)])
            return sS.n

        def ln3_back(t, sq_done):
            sb = t % 2
            p.wait("vector", sS, sq_done)
            p.do("vector", lambda v, sb=sb: v.reciprocal(rin_t[sb][:, :], sd_t[sb][:, :]),
                 inc=[(sV, 1)])
            p.wait("vector", sV)
            if x3bf_free[sb] is not None:
                p.wait("vector", *x3bf_free[sb])
            p.do("vector", lambda v, t=t, sb=sb: v.tensor_scalar(
                x3bf[sb][:, :], x2v(t), stB[sb][:, 0:1], rin_t[sb][:, :],
                ALU.subtract, ALU.mult), inc=[(sV, 1)])
            apply_done = sV.n
            stats_free[sb] = (sV, apply_done)
            p.wait("tensor", sV, apply_done)
            if ptr_free[sb] is not None:
                p.wait("tensor", *ptr_free[sb])
            for cc in range(NC):
                p.do("tensor", lambda te, sb=sb, cc=cc: te.transpose(
                    ptrb[sb][:, cc * 128:(cc + 1) * 128],
                    x3bf[sb][:, cc * 128:(cc + 1) * 128], ident[:, :]),
                    inc=[(sP, 1)] if cc == NC - 1 else None)
            tr_done = sP.n
            x3bf_free[sb] = (sP, tr_done)
            p.wait("scalar", sP, tr_done)
            p.do("scalar", lambda s, sb=sb, t=t: s.copy(
                AB0[:, :, t * 128:(t + 1) * 128],
                ptrb[sb].rearrange("p (c q) -> p c q", c=NC)), inc=[(sS, 1)])
            ptr_free[sb] = (sS, sS.n)

        l3_rec = {0: ln3_front(0)}
        for t in range(NT):
            if t + 1 < NT:
                l3_rec[t + 1] = ln3_front(t + 1)
            ln3_back(t, l3_rec[t])
        x3T_done_S = sS.n
        ps_close("ptrb1", "ptrb0")

        if upto < 70:
            return
        pf = [ps_open(f"pf{i}", [128, 512], F32) for i in range(4)]
        pf_free = [None] * 4

        def ffn_fm(Wsb, srcT, dstT, w_sem, w_val, s_sem, s_val):
            p.wait("tensor", w_sem, w_val)
            p.wait("tensor", s_sem, s_val)
            for fr in range(NC):
                for cc in range(NC):
                    for tq in range(NQC):
                        if cc == 0 and pf_free[tq] is not None:
                            p.wait("tensor", *pf_free[tq])
                        p.do("tensor", lambda te, fr=fr, cc=cc, tq=tq: te.matmul(
                            pf[tq][:, :],
                            Wsb[:, cc, fr * 128:(fr + 1) * 128],
                            srcT[:, cc, tq * 512:(tq + 1) * 512],
                            start=(cc == 0), stop=(cc == NC - 1)),
                            inc=[(sP, 1)] if cc == NC - 1 else None)
                mm_done = sP.n
                for tq in range(NQC):
                    p.wait("scalar", sP, mm_done - (NQC - 1 - tq))
                    p.do("scalar", lambda s, fr=fr, tq=tq: s.copy(
                        dstT[:, fr, tq * 512:(tq + 1) * 512], pf[tq][:, :]), inc=[(sS, 1)])
                    pf_free[tq] = (sS, sS.n)
            return sS.n

        h1_done_S = ffn_fm(W1sb, AB0, AB1, sGW, W1_done, sS, x3T_done_S)
        p.wait("gpsimd", sS, h1_done_S)
        p.do("gpsimd", lambda g: g.dma_start(
            out=W3sb[:, :, :], in_=w3_d.rearrange("(cc p) f -> p cc f", p=128)),
            inc=[(sGW, 16)])
        W3_done = sGW.n
        h2_done_S = ffn_fm(W2sb, AB1, AB0, sGW, W2_done, sS, h1_done_S)
        ps_close("pf3", "pf2", "pf1", "pf0")

        pbias2 = ps_open("pbias2", [128, C], F32)
        p.wait("tensor", sS, h2_done_S)
        p.wait("tensor", sV, x2_done_V)
        p.do("tensor", lambda t: t.matmul(pbias2[:, 0:512], ones1[:, :],
                                          fbrow_bf[:, 0:512], start=True, stop=True))
        p.do("tensor", lambda t: t.matmul(pbias2[:, 512:1024], ones1[:, :],
                                          fbrow_bf[:, 512:1024], start=True, stop=True),
             inc=[(sP, 1)])
        p.wait("vector", sP)
        p.do("vector", lambda v: v.tensor_copy(bias_bc[:, :], pbias2[:, :]), inc=[(sV, 1)])
        fbbc_done = sV.n
        ps_close("pbias2")

        if upto < 80:
            return
        ph3 = [ps_open(f"ph3{i}", [128, C], F32) for i in range(2)]
        ph3_free = [None] * 2
        h3sb_free = [None] * 2
        lnf_free = [None] * 2
        outt_free = [None] * 2
        rstd_grp = {}
        aggr4 = {}
        LAG4 = 5
        p.wait("tensor", sV, fbbc_done)
        p.wait("tensor", sS, h2_done_S)
        p.wait("tensor", sGW, W3_done)

        def emit_4c(t):
            pb = t % 2
            if ph3_free[pb] is not None:
                p.wait("tensor", *ph3_free[pb])
            for cc in range(NC):
                for fc in range(2):
                    p.do("tensor", lambda te, t=t, pb=pb, cc=cc, fc=fc: te.matmul(
                        ph3[pb][:, fc * 512:(fc + 1) * 512],
                        AB0[:, cc, t * 128:(t + 1) * 128],
                        W3sb[:, cc, fc * 512:(fc + 1) * 512],
                        start=(cc == 0), stop=(cc == NC - 1)),
                        inc=[(sP, 1)] if (cc == NC - 1 and fc == 1) else None)
            mm_done = sP.n
            p.wait("vector", sP, mm_done)
            p.wait("vector", sV, fbbc_done)
            if h3sb_free[pb] is not None:
                p.wait("vector", *h3sb_free[pb])
            p.do("vector", lambda v, pb=pb: v.tensor_tensor(
                h3sb[pb][:, :], ph3[pb][:, :], bias_bc[:, :], ALU.add), inc=[(sV, 1)])
            badd_done = sV.n
            ph3_free[pb] = (sV, badd_done)
            p.wait("scalar", sV, badd_done)
            p.do("scalar", lambda s, pb=pb, t=t: s.activation(
                f_bf[:, t, :], h3sb[pb][:, :], AF.Tanh), inc=[(sS, 1)])
            tanh_done = sS.n
            h3sb_free[pb] = (sS, tanh_done)
            p.wait("vector", sS, tanh_done)
            p.do("vector", lambda v, t=t: v.bn_stats(stA[pb][:, 0:6], f_bf[:, t, 0:512]))
            p.do("vector", lambda v, t=t: v.bn_stats(stA[pb][:, 6:12], f_bf[:, t, 512:1024]),
                 inc=[(sV, 1)])
            p.wait("vector", sV)
            p.do("vector", lambda v, pb=pb, t=t: v.bn_aggr(stB_all[:, :, t], stA[pb][:, :]),
                 inc=[(sV, 1)])
            aggr4[t] = sV.n

        def emit_rstd4(g):
            p.wait("scalar", sV, aggr4[4 * g + 3])
            p.do("scalar", lambda s, g=g: s.activation(
                sd_all[:, 4 * g:4 * g + 4], stB_all[:, 1, 4 * g:4 * g + 4],
                AF.Ln, bias=eps_t[:, :]), inc=[(sS, 1)])
            p.wait("scalar", sS)
            p.do("scalar", lambda s, g=g: s.activation(
                rin_all[:, 4 * g:4 * g + 4], sd_all[:, 4 * g:4 * g + 4],
                AF.Exp, scale=-0.5), inc=[(sS, 1)])
            rstd_grp[g] = sS.n

        def emit_4d(t):
            pb = t % 2
            p.wait("vector", sS, rstd_grp[t // 4])
            if lnf_free[pb] is not None:
                p.wait("vector", *lnf_free[pb])
            p.do("vector", lambda v, pb=pb, t=t: v.tensor_scalar(
                lnf[pb][:, :], f_bf[:, t, :], stB_all[:, 0, t:t + 1], rin_all[:, t:t + 1],
                ALU.subtract, ALU.mult), inc=[(sV, 1)])
            lnf_done = sV.n
            ob = (t // 2) % 2
            u = t % 2
            p.wait("gpsimd", sV, lnf_done)
            if u == 0 and outt_free[ob] is not None:
                p.wait("gpsimd", *outt_free[ob])
            p.do("gpsimd", lambda g, ob=ob, u=u, pb=pb, t=t: g.tensor_tensor(
                outt[ob][:, u, :], lnf[pb][:, :], x2v(t), ALU.add), inc=[(sG, 1)])
            lnf_free[pb] = (sG, sG.n)
            if u == 1:
                tp2 = t // 2
                dst = out_d[tp2 * 256:(tp2 + 1) * 256, :].rearrange("(w p) c -> p w c", p=128)
                p.wait("gpsimd", sG, sG.n)
                p.do("gpsimd", lambda g, ob=ob, dst=dst: g.dma_start(
                    out=dst, in_=outt[ob][:, :, :]), inc=[(sO[ob], 16)])
                outt_free[ob] = (sO[ob], sO[ob].n)

        for t in range(NT):
            emit_4c(t)
            if t % 4 == 3:
                emit_rstd4(t // 4)
            if t >= LAG4:
                emit_4d(t - LAG4)
        for t in range(NT - LAG4, NT):
            emit_4d(t)
        ln4_stats_V = sV.n
        ps_close("ph31", "ph30")

        if debug:
            dump("st4", stB_all[:, :, :], sV, ln4_stats_V)
            dump("fbf", f_bf[:, :, :], sV, ln4_stats_V)
        p.wait("gpsimd", sO0, sO0.n)
        p.wait("gpsimd", sO1, sO1.n)

    @blk.sync
    def _(sync):
        walk(P("sync", sync, sems))
        walk_state["first"] = False

    @blk.gpsimd
    def _(gpsimd):
        walk(P("gpsimd", gpsimd, sems))

    @blk.tensor
    def _(tensor):
        walk(P("tensor", tensor, sems))

    @blk.vector
    def _(vector):
        walk(P("vector", vector, sems))

    @blk.scalar
    def _(scalar):
        walk(P("scalar", scalar, sems))

    return nc


def kernel(L0_attn_x, L0_attn_w_qkv, L0_attn_w_proj, L0_attn_b_proj,
           L0_ffn_weight, L0_ffn_aux_weight1, L0_ffn_aux_weight2, L0_ffn_bias):
    nc = build_program()
    wqkv = np.ascontiguousarray(np.asarray(L0_attn_w_qkv, np.float32))
    wproj = np.ascontiguousarray(np.asarray(L0_attn_w_proj, np.float32))
    bproj = np.ascontiguousarray(np.asarray(L0_attn_b_proj, np.float32)).reshape(1, C)
    w1 = np.ascontiguousarray(np.asarray(L0_ffn_weight, np.float32))
    w2 = np.ascontiguousarray(np.asarray(L0_ffn_aux_weight1, np.float32))
    w3 = np.ascontiguousarray(np.asarray(L0_ffn_aux_weight2, np.float32))
    fb = np.ascontiguousarray(np.asarray(L0_ffn_bias, np.float32)).reshape(1, C)
    xs = np.asarray(L0_attn_x, np.float32)
    in_maps = []
    for b in range(8):
        in_maps.append(dict(x=np.ascontiguousarray(xs[b]), wqkv=wqkv, wproj=wproj,
                            bproj=bproj, w1=w1, w2=w2, w3=w3, fbias=fb))
    res = run_bass_kernel_spmd(nc, in_maps, core_ids=list(range(8)))
    return np.stack([np.asarray(res.results[b]["out"], np.float32) for b in range(8)], 0)


# version 40
# speedup vs baseline: 1.0623x; 1.0623x over previous
import os
import numpy as np
import ml_dtypes
from contextlib import ExitStack

import concourse.bass as bass
import concourse.mybir as mybir
from concourse.bass_utils import run_bass_kernel_spmd

F32 = mybir.dt.float32
BF16 = mybir.dt.bfloat16
AF = mybir.ActivationFunctionType
ALU = mybir.AluOpType

T = 2048
C = 1024
H = 16
HS = 64
NT = T // 128
NC = C // 128
NQC = T // 512
EPS = 1e-5


class Sem:
    def __init__(self, ctx, nc, name):
        self.h = ctx.enter_context(nc.semaphore(name))
        self.n = 0


class P:

    def __init__(self, target, eng, sems):
        self.target = target
        self.eng = eng
        self.sems = sems
        for s in sems.values():
            s.n = 0
        self._waited = {}

    def do(self, engine, fn, inc=None):
        if engine == self.target:
            inst = fn(self.eng)
            if inc is not None:
                for s, k in inc:
                    inst.then_inc(s.h, k)
        if inc is not None:
            for s, k in inc:
                s.n += k

    def wait(self, engine, sem, val=None):
        v = sem.n if val is None else val
        if v <= 0:
            return
        if engine == self.target:
            key = id(sem)
            if self._waited.get(key, -1) >= v:
                return
            self._waited[key] = v
            self.eng.wait_ge(sem.h, v)


def _carve(ar, byte_off, dtype, shape):
    esz = 2 if dtype == BF16 else 4
    n = 1
    for d in shape:
        n *= d
    v = ar[:, byte_off // 2: byte_off // 2 + (n * esz) // 2]
    if dtype != BF16:
        v = v.bitcast(dtype)
    if len(shape) == 1:
        return v
    spec = "p (" + " ".join(f"d{i}" for i in range(len(shape))) + ") -> p " + \
           " ".join(f"d{i}" for i in range(len(shape)))
    kw = {f"d{i}": shape[i] for i in range(len(shape))}
    return v.rearrange(spec, **kw)


def build_program(debug=False, upto=99):
    nc = bass.Bass("TRN2", target_bir_lowering=False)
    ctx = ExitStack()

    di = lambda n, s, d: nc.dram_tensor(n, s, d, kind="ExternalInput").ap()
    x_d = di("x", [T, C], F32)
    wqkv_d = di("wqkv", [H, 3 * HS, C], F32)
    wproj_d = di("wproj", [C, C], F32)
    bproj_d = di("bproj", [1, C], F32)
    w1_d = di("w1", [C, C], F32)
    w2_d = di("w2", [C, C], F32)
    w3_d = di("w3", [C, C], F32)
    fbias_d = di("fbias", [1, C], F32)
    out_d = nc.dram_tensor("out", [T, C], F32, kind="ExternalOutput").ap()

    wqkv_bf_d = nc.dram_tensor("wqkv_bf", [3 * C, C], BF16, kind="Internal").ap()
    wproj_bf_d = nc.dram_tensor("wproj_bf", [C, C], BF16, kind="Internal").ap()
    dbg = {}
    if debug:
        dd = lambda n, s_, d: nc.dram_tensor(n, s_, d, kind="ExternalOutput").ap()
        dbg["x1T"] = dd("dbg_x1T", [128, NC, T], BF16)
        dbg["v65"] = dd("dbg_v65", [128, NT * H * 65], BF16)
        dbg["oT"] = dd("dbg_oT", [128, NC, T], BF16)
        dbg["x2a"] = dd("dbg_x2a", [128, NT // 2, C], F32)
        dbg["x2b"] = dd("dbg_x2b", [128, NT // 2, C], F32)
        dbg["st4"] = dd("dbg_st4", [128, 2, NT], F32)
        dbg["fbf"] = dd("dbg_fbf", [128, NT, C], BF16)

    x_t = x_d.rearrange("(t p) c -> t p c", p=128)
    out_t = out_d.rearrange("(t p) c -> t p c", p=128)

    sbr = lambda n, s, d: ctx.enter_context(nc.sbuf_tensor(n, s, d, side="right"))
    ident = sbr("ident", [128, 128], BF16)
    trineg = sbr("trineg", [128, 128], F32)
    ones1 = sbr("ones1", [1, 128], BF16)
    bprow_bf = sbr("bprow_bf", [1, C], BF16)
    fbrow_bf = sbr("fbrow_bf", [1, C], BF16)
    bias_bc = sbr("bias_bc", [128, C], F32)
    stA = [sbr(f"stA{i}", [128, 12], F32) for i in range(2)]
    stB = [sbr(f"stB{i}", [128, 2], F32) for i in range(2)]
    sd_t = [sbr(f"sd{i}", [128, 1], F32) for i in range(2)]
    rin_t = [sbr(f"rin{i}", [128, 1], F32) for i in range(2)]
    stB_all = sbr("stB_all", [128, 2, NT], F32)
    sd_all = sbr("sd_all", [128, NT], F32)
    rin_all = sbr("rin_all", [128, NT], F32)
    lnl_bf = [sbr(f"lnl{i}", [1, 512], BF16) for i in range(2)]
    lbc_sb = [sbr(f"lbc{i}", [64, 512], BF16) for i in range(2)]
    oTun = [sbr(f"oTun{i}", [65, 512], BF16) for i in range(2)]
    eps_t = sbr("eps_t", [128, 1], F32)
    qscale_t = sbr("qscale_t", [128, 1], F32)

    KB = 1024
    arA = ctx.enter_context(nc.sbuf_tensor("arA", [128, 28 * KB], BF16)).ap()
    arB = ctx.enter_context(nc.sbuf_tensor("arB", [128, 16 * KB], BF16)).ap()
    arC = ctx.enter_context(nc.sbuf_tensor("arC", [128, 16 * KB + 256], BF16)).ap()
    arD = ctx.enter_context(nc.sbuf_tensor("arD", [128, 16 * KB], BF16)).ap()
    arE = ctx.enter_context(nc.sbuf_tensor("arE", [128, 16 * KB], BF16)).ap()

    wqkvT = _carve(arA, 0, BF16, [NC, 3 * C])
    xa = [_carve(arD, i * 16 * KB, F32, [4, C]) for i in range(2)]
    x1bf = [_carve(arE, i * 2 * KB, BF16, [C]) for i in range(2)]
    pTt = [_carve(arA, 48 * KB + i * 2 * KB, BF16, [1024]) for i in range(3)]
    x3bf = [_carve(arA, 48 * KB + i * 2 * KB, BF16, [C]) for i in range(2)]
    wprojT = _carve(arA, 0, BF16, [NC, C])
    W1sb = _carve(arA, 16 * KB, BF16, [NC, C])
    W2sb = _carve(arA, 32 * KB, BF16, [NC, C])
    W3sb = _carve(arA, 0, BF16, [NC, C])
    xa2 = [_carve(arA, 48 * KB + i * 4 * KB, F32, [C]) for i in range(2)]
    h3sb = [_carve(arA, 16 * KB + i * 4 * KB, F32, [C]) for i in range(2)]
    fsb = [_carve(arA, 24 * KB + i * 4 * KB, F32, [C]) for i in range(2)]
    lnf = [_carve(arA, 48 * KB + i * 4 * KB, F32, [C]) for i in range(2)]
    outt = [_carve(arA, 32 * KB + i * 8 * KB, F32, [2, C]) for i in range(2)]

    x1T = _carve(arB, 0, BF16, [NC, T])
    x2b = _carve(arB, 0, F32, [NT // 2, C])

    v65 = _carve(arC, 0, BF16, [NT, H * 65])
    o2sb = [_carve(arC, i * 4 * KB, F32, [C]) for i in range(2)]
    lnout = [_carve(arC, 8 * KB + i * 4 * KB, F32, [C]) for i in range(2)]
    AB1 = _carve(arC, 0, BF16, [NC, T])
    f_bf = _carve(arC, 0, BF16, [NT, C])

    qkT = [_carve(arD, j * 8 * KB, BF16, [2, T]) for j in range(2)]
    x2a = _carve(arD, 0, F32, [NT // 2, C])
    oT = _carve(arE, 0, BF16, [NC, T])
    AB0 = _carve(arE, 0, BF16, [NC, T])

    def x2v(t):
        return x2a[:, t, :] if t < NT // 2 else x2b[:, t - NT // 2, :]

    v65v = v65.rearrange("p t (h e) -> p t h e", h=H)
    wqkvT3 = wqkvT.rearrange("p c (w r) -> p c w r", w=3)

    sems = {}
    for name in ["sLx0", "sLx1", "sLW", "sLW2", "sLP", "sG", "sGB", "sGQ", "sGQ2",
                 "sGPc", "sGW", "sP", "sV", "sS", "sO0", "sO1", "sDG"]:
        sems[name] = Sem(ctx, nc, name)

    guards = {}
    alloc_cache = {}
    walk_state = {"first": True}

    def ps_open(name, shape, dtype):
        if walk_state["first"]:
            g = nc.psum_tensor(name, shape, dtype)
            tt = g.__enter__()
            guards[name] = g
            alloc_cache[name] = tt
        return alloc_cache[name]

    def ps_close(*names):
        if walk_state["first"]:
            for name in names:
                guards.pop(name).__exit__(None, None, None)

    blk = ctx.enter_context(nc.Block())

    def walk(p: P):
        def dump(key, src_ap, wait_sem=None, wait_val=None, after_inc=None):
            if not debug:
                return
            if wait_sem is not None:
                p.wait("gpsimd", wait_sem, wait_val)
            p.do("gpsimd", lambda g: g.dma_start(out=dbg[key], in_=src_ap),
                 inc=[(p.sems["sDG"], 16)])
            p.wait("gpsimd", p.sems["sDG"])
            if after_inc is not None:
                p.do("gpsimd", lambda g: g.engine_nop(), inc=[(after_inc, 1)])
        (sLx0, sLx1, sLW, sLW2, sLP, sG, sGB, sGQ, sGQ2, sGPc, sGW, sP, sV, sS,
         sO0, sO1) = (
            p.sems[k] for k in
            ["sLx0", "sLx1", "sLW", "sLW2", "sLP", "sG", "sGB", "sGQ", "sGQ2",
             "sGPc", "sGW", "sP", "sV", "sS", "sO0", "sO1"])
        sLx = [sLx0, sLx1]
        sO = [sO0, sO1]

        p.do("gpsimd", lambda g: g.memset(ident[:, :], 0.0))
        p.do("gpsimd", lambda g: g.memset(trineg[:, :], 0.0))
        p.do("gpsimd", lambda g: g.memset(ones1[:, :], 1.0))
        p.do("gpsimd", lambda g: g.memset(eps_t[:, :], EPS))
        p.do("gpsimd", lambda g: g.memset(qscale_t[:, :], 0.125), inc=[(sG, 1)])
        p.wait("gpsimd", sG)
        p.do("gpsimd", lambda g: g.affine_select(
            out=ident[:, :], in_=ident[:, :], compare_op=ALU.not_equal, fill=1.0,
            base=0, pattern=[[-1, 128]], channel_multiplier=1))
        p.do("gpsimd", lambda g: g.affine_select(
            out=trineg[:, :], in_=trineg[:, :], compare_op=ALU.is_ge, fill=-1e9,
            base=0, pattern=[[1, 128]], channel_multiplier=-1), inc=[(sG, 1)])
        const_done = sG.n
        p.do("gpsimd", lambda g: g.dma_start(out=bprow_bf[:, :], in_=bproj_d),
             inc=[(sGB, 16)])
        p.do("gpsimd", lambda g: g.dma_start(out=fbrow_bf[:, :], in_=fbias_d),
             inc=[(sGB, 16)])
        bias_bf_done = sGB.n

        p.wait("scalar", sG, const_done)
        p.wait("vector", sG, const_done)

        pbias = ps_open("pbias", [128, C], F32)
        p.wait("tensor", sGB, bias_bf_done)
        p.do("tensor", lambda t: t.matmul(pbias[:, 0:512], ones1[:, :],
                                          bprow_bf[:, 0:512], start=True, stop=True))
        p.do("tensor", lambda t: t.matmul(pbias[:, 512:1024], ones1[:, :],
                                          bprow_bf[:, 512:1024], start=True, stop=True),
             inc=[(sP, 1)])
        p.wait("vector", sP)
        p.do("vector", lambda v: v.tensor_copy(bias_bc[:, :], pbias[:, :]), inc=[(sV, 1)])
        bpbc_done = sV.n
        ps_close("pbias")

        if upto < 10:
            return
        ptr = [ps_open(f"ptr{i}", [128, C], BF16) for i in range(2)]
        xa_free = [None] * 2
        x1bf_free = [None] * 2
        ptr_free = [None] * 2
        stats_free = [None] * 2
        wqkvT_done = 0
        def p1_load2(tp4):
            b = tp4 % 2
            if xa_free[b] is not None:
                p.wait("sync", *xa_free[b])
            src = x_d[tp4 * 512:(tp4 + 1) * 512, :].rearrange("(u p) c -> p u c", p=128)
            nsp = 4
            uw = 4 // nsp
            for i in range(nsp):
                p.do("sync", lambda s, b=b, src=src, i=i, uw=uw: s.dma_start(
                    out=xa[b][:, i * uw:(i + 1) * uw, :],
                    in_=src[:, i * uw:(i + 1) * uw, :]), inc=[(sLx[b], 16)])
            return (sLx[b], sLx[b].n)

        def p1_stats(t, x_loaded):
            sb = t % 2
            u = t % 4
            b = (t // 4) % 2
            p.wait("vector", *x_loaded)
            if stats_free[sb] is not None:
                p.wait("vector", *stats_free[sb])
            p.do("vector", lambda v, b=b, u=u, sb=sb: v.bn_stats(stA[sb][:, 0:6], xa[b][:, u, 0:512]))
            p.do("vector", lambda v, b=b, u=u, sb=sb: v.bn_stats(stA[sb][:, 6:12], xa[b][:, u, 512:1024]),
                 inc=[(sV, 1)])
            p.wait("vector", sV)
            p.do("vector", lambda v, sb=sb: v.bn_aggr(stB[sb][:, :], stA[sb][:, :]), inc=[(sV, 1)])
            aggr_done = sV.n
            p.wait("scalar", sV, aggr_done)
            p.do("scalar", lambda s, sb=sb: s.activation(sd_t[sb][:, :], stB[sb][:, 1:2],
                                                         AF.Sqrt, bias=eps_t[:, :]), inc=[(sS, 1)])
            return sS.n

        def p1_apply(t, sq_done):
            sb = t % 2
            u = t % 4
            b = (t // 4) % 2
            p.wait("vector", sS, sq_done)
            p.do("vector", lambda v, sb=sb: v.reciprocal(rin_t[sb][:, :], sd_t[sb][:, :]),
                 inc=[(sV, 1)])
            p.wait("vector", sV)
            if x1bf_free[sb] is not None:
                p.wait("vector", *x1bf_free[sb])
            p.do("vector", lambda v, b=b, u=u, sb=sb: v.tensor_scalar(
                x1bf[sb][:, :], xa[b][:, u, :], stB[sb][:, 0:1], rin_t[sb][:, :],
                ALU.subtract, ALU.mult), inc=[(sV, 1)])
            apply_done = sV.n
            if t % 4 == 3:
                xa_free[b] = (sV, apply_done)
            stats_free[sb] = (sV, apply_done)
            p.wait("tensor", sV, apply_done)
            p.wait("tensor", sG, const_done)
            if ptr_free[sb] is not None:
                p.wait("tensor", *ptr_free[sb])
            for cc in range(NC):
                p.do("tensor", lambda te, sb=sb, cc=cc: te.transpose(
                    ptr[sb][:, cc * 128:(cc + 1) * 128],
                    x1bf[sb][:, cc * 128:(cc + 1) * 128], ident[:, :]),
                    inc=[(sP, 1)] if cc == NC - 1 else None)
            tr_done = sP.n
            x1bf_free[sb] = (sP, tr_done)
            p.wait("scalar", sP, tr_done)
            p.do("scalar", lambda s, sb=sb, t=t: s.copy(
                x1T[:, :, t * 128:(t + 1) * 128],
                ptr[sb].rearrange("p (c q) -> p c q", c=NC)), inc=[(sS, 1)])
            ptr_free[sb] = (sS, sS.n)

        ld_rec = {}
        sq_rec = {}
        rec = {}
        ld_rec[0] = p1_load2(0)
        ld_rec[1] = p1_load2(1)
        sq_rec[0] = p1_stats(0, ld_rec[0])
        for hg in range(4):
            p.do("gpsimd", lambda g, hg=hg: g.dma_start(
                out=wqkv_bf_d[2 * C + hg * 256:2 * C + (hg + 1) * 256, :]
                    .rearrange("(h d) c -> h d c", h=4),
                in_=wqkv_d[hg * 4:(hg + 1) * 4, 2 * HS:3 * HS, :]), inc=[(sGQ2, 16)])
        wqkv2_cast_done = sGQ2.n
        p.do("gpsimd", lambda g: g.memset(v65v[:, :, :, 64:65], 1.0), inc=[(sG, 1)])
        v65ones_done = sG.n

        p.wait("sync", sGQ2, wqkv2_cast_done)
        for cc in range(NC):
            p.do("sync", lambda s, cc=cc: s.dma_start_transpose(
                out=wqkvT3[:, cc, 2, :],
                in_=wqkv_bf_d[2 * C:3 * C, cc * 128:(cc + 1) * 128]),
                inc=[(sLW2, 16)])
        rec["wqkvT2_done"] = sLW2.n

        p.wait("gpsimd", sLW2, rec["wqkvT2_done"])
        for w in (0, 1):
            for hh in range(2):
                p.do("gpsimd", lambda g, w=w, hh=hh: g.dma_start(
                    out=wqkv_bf_d[w * C + hh * 512:w * C + (hh + 1) * 512, :]
                        .rearrange("(h d) c -> h d c", h=8),
                    in_=wqkv_d[hh * 8:(hh + 1) * 8, w * HS:(w + 1) * HS, :]),
                    inc=[(sGQ, 16)])
        wqkv_cast_done = sGQ.n

        pv = [ps_open(f"pv{i}", [128, C], F32) for i in range(2)]
        pv_free = [None] * 2
        x1T_tile = {}
        VSKEW = 6

        def emit_v(t):
            pb = t % 2
            if pv_free[pb] is not None:
                p.wait("tensor", *pv_free[pb])
            p.wait("tensor", sS, x1T_tile[t])
            p.wait("tensor", sLW2, rec["wqkvT2_done"])
            for half in range(2):
                for cc in range(NC):
                    p.do("tensor", lambda te, t=t, pb=pb, half=half, cc=cc: te.matmul(
                        pv[pb][:, half * 512:(half + 1) * 512],
                        x1T[:, cc, t * 128:(t + 1) * 128],
                        wqkvT3[:, cc, 2, half * 512:(half + 1) * 512],
                        start=(cc == 0), stop=(cc == NC - 1)),
                        inc=[(sP, 1)] if (cc == NC - 1 and half == 1) else None)
            v_mm_done = sP.n
            p.wait("scalar", sP, v_mm_done)
            p.do("scalar", lambda s, t=t, pb=pb: s.copy(
                v65v[:, t, :, 0:64],
                pv[pb].rearrange("p (h e) -> p h e", h=H)), inc=[(sS, 1)])
            pv_free[pb] = (sS, sS.n)

        for t in range(NT):
            g = t // 4 + 1
            if t % 4 == 0 and 2 <= g < NT // 4:
                ld_rec[g] = p1_load2(g)
            if t + 1 < NT:
                sq_rec[t + 1] = p1_stats(t + 1, ld_rec[(t + 1) // 4])
            p1_apply(t, sq_rec[t])
            x1T_tile[t] = sS.n
            if t >= VSKEW:
                emit_v(t - VSKEW)
        for t in range(NT - VSKEW, NT):
            emit_v(t)
        x1T_done_S = x1T_tile[NT - 1]
        v_done_S = sS.n

        p.wait("sync", sGQ, wqkv_cast_done)
        for w in range(2):
            for cc in range(NC):
                p.do("sync", lambda s, w=w, cc=cc: s.dma_start_transpose(
                    out=wqkvT3[:, cc, w, :],
                    in_=wqkv_bf_d[w * C:(w + 1) * C, cc * 128:(cc + 1) * 128]),
                    inc=[(sLW, 16)])
        rec["wqkvT_done"] = sLW.n
        ps_close("pv1", "pv0", "ptr1", "ptr0")

        if upto < 20:
            return
        dump("x1T", x1T[:, :, :], sS, x1T_done_S)
        p.wait("gpsimd", sS, x1T_done_S)
        p.do("gpsimd", lambda g: g.dma_start(out=wproj_bf_d, in_=wproj_d),
             inc=[(sGPc, 16)])
        wproj_cast_done = sGPc.n

        if upto < 30:
            return
        pqk = [ps_open(f"pqk{i}", [128, 512], F32) for i in range(2)]
        pqk_free = [None] * 2
        p.wait("tensor", sS, v_done_S)
        filler_done = {}
        qk_rec = {}
        rec2 = {}

        def qk_units(pnext):
            slot = pnext % 2
            for which in range(2):
                for tq in range(NQC):
                    alt = (which * NQC + tq) % 2
                    for cc in range(NC):
                        def mm(cc=cc, which=which, tq=tq, alt=alt, pnext=pnext):
                            if cc == 0 and pqk_free[alt] is not None:
                                p.wait("tensor", *pqk_free[alt])
                            p.do("tensor", lambda te, cc=cc, which=which, tq=tq, alt=alt, pnext=pnext: te.matmul(
                                pqk[alt][:, :],
                                wqkvT3[:, cc, which, pnext * 128:(pnext + 1) * 128],
                                x1T[:, cc, tq * 512:(tq + 1) * 512],
                                start=(cc == 0), stop=(cc == NC - 1)),
                                inc=[(sP, 1)] if cc == NC - 1 else None)
                            if cc == NC - 1:
                                qk_rec[(pnext, which, tq)] = sP.n
                        yield mm

                    def evac(which=which, tq=tq, alt=alt, slot=slot, pnext=pnext):
                        p.wait("vector", sP, qk_rec[(pnext, which, tq)])
                        if which == 0:
                            p.do("vector", lambda v, alt=alt, slot=slot, tq=tq: v.tensor_scalar_mul(
                                qkT[slot][:, 0, tq * 512:(tq + 1) * 512],
                                pqk[alt][:, :], qscale_t[:, :]), inc=[(sV, 1)])
                        else:
                            p.do("vector", lambda v, alt=alt, slot=slot, tq=tq: v.tensor_copy(
                                qkT[slot][:, 1, tq * 512:(tq + 1) * 512],
                                pqk[alt][:, :]), inc=[(sV, 1)])
                        pqk_free[alt] = (sV, sV.n)
                    yield evac

        p.wait("tensor", sLW, rec["wqkvT_done"])
        for u in qk_units(0):
            u()
        filler_done[0] = (sV, sV.n)

        NPAR = 2
        ps_s = [ps_open(f"pss{i}", [128, 1024], F32) for i in range(NPAR)]
        ps_o = [ps_open(f"po{h}", [128, 512], F32) for h in range(2)]
        pT = pTt
        ps_s_free = [None] * NPAR
        NPT = 3
        pT_free = [None] * NPT
        ps_o_free = [None, None]
        lbc_free = [None, None]
        lnl_free = [None, None]
        oTun_free = [None, None]
        tail_stash = [None]

        p.wait("tensor", sG, v65ones_done)

        tail_q = []
        ucnt = [0]
        gkc_c = [0]
        cur_fil = [iter(())]

        def bcast_unit(h, lnl_val, pp, qc, alt):
            p.wait("tensor", sS, lnl_val)
            if lbc_free[h] is not None:
                p.wait("tensor", *lbc_free[h])
            if pqk_free[alt] is not None:
                p.wait("tensor", *pqk_free[alt])
            p.do("tensor", lambda te, h=h, alt=alt: te.matmul(
                pqk[alt][0:64, :], ones1[:, 0:64], lnl_bf[h][:, :],
                start=True, stop=True), inc=[(sP, 1)])
            lbc_mm = sP.n
            lnl_free[h] = (sP, lbc_mm)
            p.wait("scalar", sP, lbc_mm)
            p.do("scalar", lambda s, h=h, alt=alt: s.activation(
                lbc_sb[h][:, :], pqk[alt][0:64, :], AF.Exp, scale=-1.0), inc=[(sS, 1)])
            lbcexp_done = sS.n
            pqk_free[alt] = (sS, lbcexp_done)
            lo = h * 64
            p.wait("vector", sS, lbcexp_done)
            p.do("vector", lambda v, h=h, pp=pp, qc=qc, lo=lo: v.tensor_tensor(
                oT[lo:lo + 64, pp, qc * 512:(qc + 1) * 512],
                oTun[h][0:64, :], lbc_sb[h][:, :], ALU.mult), inc=[(sV, 1)])
            done = sV.n
            oTun_free[h] = (sV, done)
            lbc_free[h] = (sV, done)

        def pop_bcast():
            tail_q.pop(0)(((ucnt[0] // 9) + 1) % 2)

        def flush_tail_q():
            while tail_q:
                if ucnt[0] % 9 == 0:
                    pop_bcast()
                    continue
                u = next(cur_fil[0], None)
                if u is None:
                    pop_bcast()
                else:
                    u()
                    ucnt[0] += 1

        def emit_tail():
            if tail_stash[0] is None:
                return
            flush_tail_q()
            pp, qc, ov_get = tail_stash[0]
            ov_done = ov_get()
            tail_stash[0] = None
            for h in range(2):
                po = ps_o[h]
                p.wait("vector", sP, ov_done)
                if oTun_free[h] is not None:
                    p.wait("vector", *oTun_free[h])
                p.do("vector", lambda v, h=h, po=po: v.tensor_copy(oTun[h][:, :], po[0:65, :]),
                     inc=[(sV, 1)])
                otun_done = sV.n
                ps_o_free[h] = (sV, otun_done)
                p.wait("scalar", sV, otun_done)
                if lnl_free[h] is not None:
                    p.wait("scalar", *lnl_free[h])
                p.do("scalar", lambda s, h=h: s.activation(
                    lnl_bf[h][:, :], oTun[h][64:65, :], AF.Ln), inc=[(sS, 1)])
                lnl_done = sS.n
                tail_q.append(lambda alt, h=h, lnl_done=lnl_done, pp=pp, qc=qc: bcast_unit(
                    h, lnl_done, pp, qc, alt))

        for pp in range(8):
            slot = pp % 2
            cur_fil[0] = qk_units(pp + 1) if pp < 7 else iter(())

            def fill(n):
                for _ in range(n):
                    if ucnt[0] % 9 == 0 and tail_q:
                        pop_bcast()
                    u = next(cur_fil[0], None)
                    if u is None:
                        return
                    u()
                    ucnt[0] += 1

            p.wait("tensor", *filler_done[pp])
            exp_rec = {}
            pend = []
            last_av_P = {}

            def emit_av(ent, pp=pp, exp_rec=exp_rec, last_av_P=last_av_P):
                qc_, nkc_, kc, par = ent
                sem, val = exp_rec[(qc_, kc)]
                p.wait("tensor", sem, val)
                for h in range(2):
                    if kc == 0 and ps_o_free[h] is not None:
                        p.wait("tensor", *ps_o_free[h])
                    p.do("tensor", lambda te, pp=pp, kc=kc, h=h, par=par, nkc_=nkc_: te.matmul(
                        ps_o[h][0:65, :],
                        v65v[:, kc, 2 * pp + h, :],
                        pT[par][:, h * 512:(h + 1) * 512],
                        start=(kc == 0), stop=(kc == nkc_ - 1)), inc=[(sP, 1)])
                pT_free[par] = (sP, sP.n)
                last_av_P[qc_] = sP.n

            for qc in range(NQC):
                nkc = 4 * qc + 4

                for kc in range(nkc):
                    par = kc % NPAR
                    parp = gkc_c[0] % NPT
                    gkc_c[0] += 1
                    diag = kc >= 4 * qc
                    dq = 128 * (kc - 4 * qc) if diag else 0
                    if ps_s_free[par] is not None:
                        p.wait("tensor", *ps_s_free[par])
                    for h in range(2):
                        lo = h * 64
                        p.do("tensor", lambda te, pp=pp, qc=qc, kc=kc, h=h, par=par, lo=lo, slot=slot: te.matmul(
                            ps_s[par][:, h * 512:(h + 1) * 512],
                            qkT[slot][lo:lo + 64, 1, kc * 128:(kc + 1) * 128],
                            qkT[slot][lo:lo + 64, 0, qc * 512:(qc + 1) * 512],
                            start=True, stop=True),
                            inc=[(sP, 1)] if h == 1 else None)
                    s_done = sP.n
                    if len(pend) > 2:
                        emit_av(pend.pop(0))
                    if kc == 2:
                        emit_tail()
                    fill(2)
                    if diag:
                        p.wait("vector", sP, s_done)
                        for h in range(2):
                            p.do("vector", lambda v, h=h, par=par, dq=dq: v.tensor_tensor(
                                ps_s[par][:, h * 512 + dq:h * 512 + dq + 128],
                                ps_s[par][:, h * 512 + dq:h * 512 + dq + 128],
                                trineg[:, :], ALU.add),
                                inc=[(sV, 1)] if h == 1 else None)
                        smask_done = sV.n
                    if diag:
                        p.wait("scalar", sV, smask_done)
                    else:
                        p.wait("scalar", sP, s_done)
                    if pT_free[parp] is not None:
                        p.wait("scalar", *pT_free[parp])
                    p.do("scalar", lambda s, par=par, parp=parp, dq=dq: s.activation(
                        pT[parp][:, dq:1024], ps_s[par][:, dq:1024], AF.Exp), inc=[(sS, 1)])
                    ps_s_free[par] = (sS, sS.n)
                    gate = (sS, sS.n)
                    if dq > 0:
                        p.wait("gpsimd", sS, gate[1])
                        p.do("gpsimd", lambda g, parp=parp, dq=dq: g.memset(
                            pT[parp][:, 0:dq], 0.0))
                        p.do("gpsimd", lambda g, parp=parp, dq=dq: g.memset(
                            pT[parp][:, 512:512 + dq], 0.0), inc=[(sG, 1)])
                        gate = (sG, sG.n)
                    exp_rec[(qc, kc)] = gate
                    pend.append((qc, nkc, kc, parp))
                    fill(1)
                tail_stash[0] = (pp, qc, (lambda d=last_av_P, q=qc: d[q]))
                if qc == 2:
                    fill(10 ** 6)
            fill(10 ** 6)
            for ent in pend:
                emit_av(ent)
            pend.clear()
            if pp < 7:
                filler_done[pp + 1] = (sV, sV.n)
            if pp == 6:
                rec2["fil7_P"] = sP.n
        emit_tail()
        flush_tail_q()
        attn_done_V = sV.n
        ps_close("po1", "po0", "pss1", "pss0", "pqk1", "pqk0")

        p.wait("sync", sGPc, wproj_cast_done)
        p.wait("sync", sP, rec2["fil7_P"])
        for cc in range(NC):
            p.do("sync", lambda s, cc=cc: s.dma_start_transpose(
                out=wprojT[:, cc, :], in_=wproj_bf_d[:, cc * 128:(cc + 1) * 128]),
                inc=[(sLP, 16)])
        wprojT_done = sLP.n
        p.wait("gpsimd", sP, rec2["fil7_P"])
        p.do("gpsimd", lambda g: g.dma_start(
            out=W1sb[:, :, :], in_=w1_d.rearrange("(cc p) f -> p cc f", p=128)),
            inc=[(sGW, 16)])
        p.do("gpsimd", lambda g: g.dma_start(
            out=W2sb[:, :, :], in_=w2_d.rearrange("(cc p) f -> p cc f", p=128)),
            inc=[(sGW, 16)])
        W1_done = sGW.n
        W2_done = sGW.n

        if upto < 50:
            return
        oT_dumped = Sem.__new__(Sem); oT_dumped = None
        if debug:
            dump("oT", oT[:, :, :], sV, attn_done_V, after_inc=sG)
            oT_dump_done = sG.n
        po2 = [ps_open(f"po2{i}", [128, C], F32) for i in range(2)]
        po2_free = [None] * 2
        o2_free = [None] * 2
        ln_free = [None] * 2
        stats_free = [None] * 2
        xa2_free = [None] * 2
        p.wait("sync", sV, attn_done_V)
        p.wait("tensor", sV, attn_done_V)
        p.wait("tensor", sLP, wprojT_done)
        p.wait("tensor", sV, bpbc_done)

        def p3_front(t):
            pb = t % 2
            if xa2_free[pb] is not None:
                p.wait("sync", *xa2_free[pb])
            p.do("sync", lambda s, t=t, pb=pb: s.dma_start(out=xa2[pb][:, :], in_=x_t[t]),
                 inc=[(sLx[pb], 16)])
            x2_loaded = (sLx[pb], sLx[pb].n)
            if po2_free[pb] is not None:
                p.wait("tensor", *po2_free[pb])
            for cc in range(NC):
                for fc in range(2):
                    p.do("tensor", lambda te, t=t, pb=pb, cc=cc, fc=fc: te.matmul(
                        po2[pb][:, fc * 512:(fc + 1) * 512],
                        oT[:, cc, t * 128:(t + 1) * 128],
                        wprojT[:, cc, fc * 512:(fc + 1) * 512],
                        start=(cc == 0), stop=(cc == NC - 1)),
                        inc=[(sP, 1)] if (cc == NC - 1 and fc == 1) else None)
            mm_done = sP.n
            p.wait("vector", sP, mm_done)
            if o2_free[pb] is not None:
                p.wait("vector", *o2_free[pb])
            p.do("vector", lambda v, pb=pb: v.tensor_tensor(
                o2sb[pb][:, :], po2[pb][:, :], bias_bc[:, :], ALU.add), inc=[(sV, 1)])
            po2_free[pb] = (sV, sV.n)
            p.wait("vector", sV)
            if stats_free[pb] is not None:
                p.wait("vector", *stats_free[pb])
            p.do("vector", lambda v, pb=pb: v.bn_stats(stA[pb][:, 0:6], o2sb[pb][:, 0:512]))
            p.do("vector", lambda v, pb=pb: v.bn_stats(stA[pb][:, 6:12], o2sb[pb][:, 512:1024]),
                 inc=[(sV, 1)])
            p.wait("vector", sV)
            p.do("vector", lambda v, pb=pb: v.bn_aggr(stB[pb][:, :], stA[pb][:, :]), inc=[(sV, 1)])
            aggr_done = sV.n
            p.wait("scalar", sV, aggr_done)
            p.do("scalar", lambda s, pb=pb: s.activation(sd_t[pb][:, :], stB[pb][:, 1:2],
                                                         AF.Sqrt, bias=eps_t[:, :]), inc=[(sS, 1)])
            return (sS.n, x2_loaded)

        def p3_back(t, rec):
            pb = t % 2
            sq_done, x2_loaded = rec
            p.wait("vector", sS, sq_done)
            p.do("vector", lambda v, pb=pb: v.reciprocal(rin_t[pb][:, :], sd_t[pb][:, :]),
                 inc=[(sV, 1)])
            p.wait("vector", sV)
            if ln_free[pb] is not None:
                p.wait("vector", *ln_free[pb])
            p.do("vector", lambda v, pb=pb: v.tensor_scalar(
                lnout[pb][:, :], o2sb[pb][:, :], stB[pb][:, 0:1], rin_t[pb][:, :],
                ALU.subtract, ALU.mult), inc=[(sV, 1)])
            o2_free[pb] = (sV, sV.n)
            stats_free[pb] = (sV, sV.n)
            lnout_done = sV.n
            p.wait("gpsimd", sV, lnout_done)
            p.wait("gpsimd", *x2_loaded)
            p.do("gpsimd", lambda g, pb=pb, t=t: g.tensor_tensor(
                x2v(t), lnout[pb][:, :], xa2[pb][:, :], ALU.add), inc=[(sG, 1)])
            ln_free[pb] = (sG, sG.n)
            xa2_free[pb] = (sG, sG.n)

        p3_rec = {0: p3_front(0)}
        for t in range(NT):
            if t + 1 < NT:
                p3_rec[t + 1] = p3_front(t + 1)
            p3_back(t, p3_rec[t])
        x2_done_V = sV.n
        x2_done_G = sG.n
        ps_close("po21", "po20")

        if upto < 60:
            return
        if debug:
            dump("x2a", x2a[:, :, :], sG, x2_done_G)
            dump("x2b", x2b[:, :, :], sG, x2_done_G)
        ptrb = [ps_open(f"ptrb{i}", [128, C], BF16) for i in range(2)]
        ptr_free = [None] * 2
        x3bf_free = [None] * 2
        stats_free = [None] * 2
        p.wait("vector", sV, x2_done_V)
        p.wait("vector", sG, x2_done_G)

        def ln3_front(t):
            sb = t % 2
            if stats_free[sb] is not None:
                p.wait("vector", *stats_free[sb])
            p.do("vector", lambda v, t=t, sb=sb: v.bn_stats(stA[sb][:, 0:6], x2v(t)[:, 0:512]))
            p.do("vector", lambda v, t=t, sb=sb: v.bn_stats(stA[sb][:, 6:12], x2v(t)[:, 512:1024]),
                 inc=[(sV, 1)])
            p.wait("vector", sV)
            p.do("vector", lambda v, sb=sb: v.bn_aggr(stB[sb][:, :], stA[sb][:, :]), inc=[(sV, 1)])
            aggr_done = sV.n
            p.wait("scalar", sV, aggr_done)
            p.do("scalar", lambda s, sb=sb: s.activation(sd_t[sb][:, :], stB[sb][:, 1:2],
                                                         AF.Sqrt, bias=eps_t[:, :]), inc=[(sS, 1)])
            return sS.n

        def ln3_back(t, sq_done):
            sb = t % 2
            p.wait("vector", sS, sq_done)
            p.do("vector", lambda v, sb=sb: v.reciprocal(rin_t[sb][:, :], sd_t[sb][:, :]),
                 inc=[(sV, 1)])
            p.wait("vector", sV)
            if x3bf_free[sb] is not None:
                p.wait("vector", *x3bf_free[sb])
            p.do("vector", lambda v, t=t, sb=sb: v.tensor_scalar(
                x3bf[sb][:, :], x2v(t), stB[sb][:, 0:1], rin_t[sb][:, :],
                ALU.subtract, ALU.mult), inc=[(sV, 1)])
            apply_done = sV.n
            stats_free[sb] = (sV, apply_done)
            p.wait("tensor", sV, apply_done)
            if ptr_free[sb] is not None:
                p.wait("tensor", *ptr_free[sb])
            for cc in range(NC):
                p.do("tensor", lambda te, sb=sb, cc=cc: te.transpose(
                    ptrb[sb][:, cc * 128:(cc + 1) * 128],
                    x3bf[sb][:, cc * 128:(cc + 1) * 128], ident[:, :]),
                    inc=[(sP, 1)] if cc == NC - 1 else None)
            tr_done = sP.n
            x3bf_free[sb] = (sP, tr_done)
            p.wait("scalar", sP, tr_done)
            p.do("scalar", lambda s, sb=sb, t=t: s.copy(
                AB0[:, :, t * 128:(t + 1) * 128],
                ptrb[sb].rearrange("p (c q) -> p c q", c=NC)), inc=[(sS, 1)])
            ptr_free[sb] = (sS, sS.n)

        l3_rec = {0: ln3_front(0)}
        for t in range(NT):
            if t + 1 < NT:
                l3_rec[t + 1] = ln3_front(t + 1)
            ln3_back(t, l3_rec[t])
        x3T_done_S = sS.n
        ps_close("ptrb1", "ptrb0")

        if upto < 70:
            return
        pf = [ps_open(f"pf{i}", [128, 512], F32) for i in range(4)]
        pf_free = [None] * 4

        def ffn_fm(Wsb, srcT, dstT, w_sem, w_val, s_sem, s_val):
            p.wait("tensor", w_sem, w_val)
            p.wait("tensor", s_sem, s_val)
            for fr in range(NC):
                for cc in range(NC):
                    for tq in range(NQC):
                        if cc == 0 and pf_free[tq] is not None:
                            p.wait("tensor", *pf_free[tq])
                        p.do("tensor", lambda te, fr=fr, cc=cc, tq=tq: te.matmul(
                            pf[tq][:, :],
                            Wsb[:, cc, fr * 128:(fr + 1) * 128],
                            srcT[:, cc, tq * 512:(tq + 1) * 512],
                            start=(cc == 0), stop=(cc == NC - 1)),
                            inc=[(sP, 1)] if cc == NC - 1 else None)
                mm_done = sP.n
                for tq in range(NQC):
                    p.wait("scalar", sP, mm_done - (NQC - 1 - tq))
                    p.do("scalar", lambda s, fr=fr, tq=tq: s.copy(
                        dstT[:, fr, tq * 512:(tq + 1) * 512], pf[tq][:, :]), inc=[(sS, 1)])
                    pf_free[tq] = (sS, sS.n)
            return sS.n

        h1_done_S = ffn_fm(W1sb, AB0, AB1, sGW, W1_done, sS, x3T_done_S)
        p.wait("gpsimd", sS, h1_done_S)
        p.do("gpsimd", lambda g: g.dma_start(
            out=W3sb[:, :, :], in_=w3_d.rearrange("(cc p) f -> p cc f", p=128)),
            inc=[(sGW, 16)])
        W3_done = sGW.n
        h2_done_S = ffn_fm(W2sb, AB1, AB0, sGW, W2_done, sS, h1_done_S)
        ps_close("pf3", "pf2", "pf1", "pf0")

        pbias2 = ps_open("pbias2", [128, C], F32)
        p.wait("tensor", sS, h2_done_S)
        p.wait("tensor", sV, x2_done_V)
        p.do("tensor", lambda t: t.matmul(pbias2[:, 0:512], ones1[:, :],
                                          fbrow_bf[:, 0:512], start=True, stop=True))
        p.do("tensor", lambda t: t.matmul(pbias2[:, 512:1024], ones1[:, :],
                                          fbrow_bf[:, 512:1024], start=True, stop=True),
             inc=[(sP, 1)])
        p.wait("vector", sP)
        p.do("vector", lambda v: v.tensor_copy(bias_bc[:, :], pbias2[:, :]), inc=[(sV, 1)])
        fbbc_done = sV.n
        ps_close("pbias2")

        if upto < 80:
            return
        ph3 = [ps_open(f"ph3{i}", [128, C], F32) for i in range(2)]
        ph3_free = [None] * 2
        h3sb_free = [None] * 2
        lnf_free = [None] * 2
        outt_free = [None] * 2
        rstd_grp = {}
        aggr4 = {}
        LAG4 = 5
        p.wait("tensor", sV, fbbc_done)
        p.wait("tensor", sS, h2_done_S)
        p.wait("tensor", sGW, W3_done)

        def emit_4c(t):
            pb = t % 2
            if ph3_free[pb] is not None:
                p.wait("tensor", *ph3_free[pb])
            for cc in range(NC):
                for fc in range(2):
                    p.do("tensor", lambda te, t=t, pb=pb, cc=cc, fc=fc: te.matmul(
                        ph3[pb][:, fc * 512:(fc + 1) * 512],
                        AB0[:, cc, t * 128:(t + 1) * 128],
                        W3sb[:, cc, fc * 512:(fc + 1) * 512],
                        start=(cc == 0), stop=(cc == NC - 1)),
                        inc=[(sP, 1)] if (cc == NC - 1 and fc == 1) else None)
            mm_done = sP.n
            p.wait("vector", sP, mm_done)
            p.wait("vector", sV, fbbc_done)
            if h3sb_free[pb] is not None:
                p.wait("vector", *h3sb_free[pb])
            p.do("vector", lambda v, pb=pb: v.tensor_tensor(
                h3sb[pb][:, :], ph3[pb][:, :], bias_bc[:, :], ALU.add), inc=[(sV, 1)])
            badd_done = sV.n
            ph3_free[pb] = (sV, badd_done)
            p.wait("scalar", sV, badd_done)
            p.do("scalar", lambda s, pb=pb, t=t: s.activation(
                f_bf[:, t, :], h3sb[pb][:, :], AF.Tanh), inc=[(sS, 1)])
            tanh_done = sS.n
            h3sb_free[pb] = (sS, tanh_done)
            p.wait("vector", sS, tanh_done)
            p.do("vector", lambda v, t=t: v.bn_stats(stA[pb][:, 0:6], f_bf[:, t, 0:512]))
            p.do("vector", lambda v, t=t: v.bn_stats(stA[pb][:, 6:12], f_bf[:, t, 512:1024]),
                 inc=[(sV, 1)])
            p.wait("vector", sV)
            p.do("vector", lambda v, pb=pb, t=t: v.bn_aggr(stB_all[:, :, t], stA[pb][:, :]),
                 inc=[(sV, 1)])
            aggr4[t] = sV.n

        def emit_rstd4(g):
            p.wait("scalar", sV, aggr4[4 * g + 3])
            p.do("scalar", lambda s, g=g: s.activation(
                sd_all[:, 4 * g:4 * g + 4], stB_all[:, 1, 4 * g:4 * g + 4],
                AF.Ln, bias=eps_t[:, :]), inc=[(sS, 1)])
            p.wait("scalar", sS)
            p.do("scalar", lambda s, g=g: s.activation(
                rin_all[:, 4 * g:4 * g + 4], sd_all[:, 4 * g:4 * g + 4],
                AF.Exp, scale=-0.5), inc=[(sS, 1)])
            rstd_grp[g] = sS.n

        def emit_4d(t):
            pb = t % 2
            p.wait("vector", sS, rstd_grp[t // 4])
            if lnf_free[pb] is not None:
                p.wait("vector", *lnf_free[pb])
            p.do("vector", lambda v, pb=pb, t=t: v.tensor_scalar(
                lnf[pb][:, :], f_bf[:, t, :], stB_all[:, 0, t:t + 1], rin_all[:, t:t + 1],
                ALU.subtract, ALU.mult), inc=[(sV, 1)])
            lnf_done = sV.n
            ob = (t // 2) % 2
            u = t % 2
            p.wait("gpsimd", sV, lnf_done)
            if u == 0 and outt_free[ob] is not None:
                p.wait("gpsimd", *outt_free[ob])
            p.do("gpsimd", lambda g, ob=ob, u=u, pb=pb, t=t: g.tensor_tensor(
                outt[ob][:, u, :], lnf[pb][:, :], x2v(t), ALU.add), inc=[(sG, 1)])
            lnf_free[pb] = (sG, sG.n)
            if u == 1:
                tp2 = t // 2
                dst = out_d[tp2 * 256:(tp2 + 1) * 256, :].rearrange("(w p) c -> p w c", p=128)
                p.wait("gpsimd", sG, sG.n)
                p.do("gpsimd", lambda g, ob=ob, dst=dst: g.dma_start(
                    out=dst, in_=outt[ob][:, :, :]), inc=[(sO[ob], 16)])
                outt_free[ob] = (sO[ob], sO[ob].n)

        for t in range(NT):
            emit_4c(t)
            if t % 4 == 3:
                emit_rstd4(t // 4)
            if t >= LAG4:
                emit_4d(t - LAG4)
        for t in range(NT - LAG4, NT):
            emit_4d(t)
        ln4_stats_V = sV.n
        ps_close("ph31", "ph30")

        if debug:
            dump("st4", stB_all[:, :, :], sV, ln4_stats_V)
            dump("fbf", f_bf[:, :, :], sV, ln4_stats_V)
        p.wait("gpsimd", sO0, sO0.n)
        p.wait("gpsimd", sO1, sO1.n)

    @blk.sync
    def _(sync):
        walk(P("sync", sync, sems))
        walk_state["first"] = False

    @blk.gpsimd
    def _(gpsimd):
        walk(P("gpsimd", gpsimd, sems))

    @blk.tensor
    def _(tensor):
        walk(P("tensor", tensor, sems))

    @blk.vector
    def _(vector):
        walk(P("vector", vector, sems))

    @blk.scalar
    def _(scalar):
        walk(P("scalar", scalar, sems))

    return nc


def kernel(L0_attn_x, L0_attn_w_qkv, L0_attn_w_proj, L0_attn_b_proj,
           L0_ffn_weight, L0_ffn_aux_weight1, L0_ffn_aux_weight2, L0_ffn_bias):
    nc = build_program()
    wqkv = np.ascontiguousarray(np.asarray(L0_attn_w_qkv, np.float32))
    wproj = np.ascontiguousarray(np.asarray(L0_attn_w_proj, np.float32))
    bproj = np.ascontiguousarray(np.asarray(L0_attn_b_proj, np.float32)).reshape(1, C)
    w1 = np.ascontiguousarray(np.asarray(L0_ffn_weight, np.float32))
    w2 = np.ascontiguousarray(np.asarray(L0_ffn_aux_weight1, np.float32))
    w3 = np.ascontiguousarray(np.asarray(L0_ffn_aux_weight2, np.float32))
    fb = np.ascontiguousarray(np.asarray(L0_ffn_bias, np.float32)).reshape(1, C)
    xs = np.asarray(L0_attn_x, np.float32)
    in_maps = []
    for b in range(8):
        in_maps.append(dict(x=np.ascontiguousarray(xs[b]), wqkv=wqkv, wproj=wproj,
                            bproj=bproj, w1=w1, w2=w2, w3=w3, fbias=fb))
    res = run_bass_kernel_spmd(nc, in_maps, core_ids=list(range(8)))
    return np.stack([np.asarray(res.results[b]["out"], np.float32) for b in range(8)], 0)


# version 41
# speedup vs baseline: 1.0763x; 1.0131x over previous
import os
import numpy as np
import ml_dtypes
from contextlib import ExitStack

import concourse.bass as bass
import concourse.mybir as mybir
from concourse.bass_utils import run_bass_kernel_spmd

F32 = mybir.dt.float32
BF16 = mybir.dt.bfloat16
AF = mybir.ActivationFunctionType
ALU = mybir.AluOpType

T = 2048
C = 1024
H = 16
HS = 64
NT = T // 128
NC = C // 128
NQC = T // 512
EPS = 1e-5


class Sem:
    def __init__(self, ctx, nc, name):
        self.h = ctx.enter_context(nc.semaphore(name))
        self.n = 0


class P:

    def __init__(self, target, eng, sems):
        self.target = target
        self.eng = eng
        self.sems = sems
        for s in sems.values():
            s.n = 0
        self._waited = {}

    def do(self, engine, fn, inc=None):
        if engine == self.target:
            inst = fn(self.eng)
            if inc is not None:
                for s, k in inc:
                    inst.then_inc(s.h, k)
        if inc is not None:
            for s, k in inc:
                s.n += k

    def wait(self, engine, sem, val=None):
        v = sem.n if val is None else val
        if v <= 0:
            return
        if engine == self.target:
            key = id(sem)
            if self._waited.get(key, -1) >= v:
                return
            self._waited[key] = v
            self.eng.wait_ge(sem.h, v)


def _carve(ar, byte_off, dtype, shape):
    esz = 2 if dtype == BF16 else 4
    n = 1
    for d in shape:
        n *= d
    v = ar[:, byte_off // 2: byte_off // 2 + (n * esz) // 2]
    if dtype != BF16:
        v = v.bitcast(dtype)
    if len(shape) == 1:
        return v
    spec = "p (" + " ".join(f"d{i}" for i in range(len(shape))) + ") -> p " + \
           " ".join(f"d{i}" for i in range(len(shape)))
    kw = {f"d{i}": shape[i] for i in range(len(shape))}
    return v.rearrange(spec, **kw)


def build_program(debug=False, upto=99):
    nc = bass.Bass("TRN2", target_bir_lowering=False)
    ctx = ExitStack()

    di = lambda n, s, d: nc.dram_tensor(n, s, d, kind="ExternalInput").ap()
    x_d = di("x", [T, C], F32)
    wqkv_d = di("wqkv", [H, 3 * HS, C], F32)
    wproj_d = di("wproj", [C, C], F32)
    bproj_d = di("bproj", [1, C], F32)
    w1_d = di("w1", [C, C], F32)
    w2_d = di("w2", [C, C], F32)
    w3_d = di("w3", [C, C], F32)
    fbias_d = di("fbias", [1, C], F32)
    out_d = nc.dram_tensor("out", [T, C], F32, kind="ExternalOutput").ap()

    wqkv_bf_d = nc.dram_tensor("wqkv_bf", [3 * C, C], BF16, kind="Internal").ap()
    wproj_bf_d = nc.dram_tensor("wproj_bf", [C, C], BF16, kind="Internal").ap()
    dbg = {}
    if debug:
        dd = lambda n, s_, d: nc.dram_tensor(n, s_, d, kind="ExternalOutput").ap()
        dbg["x1T"] = dd("dbg_x1T", [128, NC, T], BF16)
        dbg["v65"] = dd("dbg_v65", [128, NT * H * 65], BF16)
        dbg["oT"] = dd("dbg_oT", [128, NC, T], BF16)
        dbg["x2a"] = dd("dbg_x2a", [128, NT // 2, C], F32)
        dbg["x2b"] = dd("dbg_x2b", [128, NT // 2, C], F32)
        dbg["st4"] = dd("dbg_st4", [128, 2, NT], F32)
        dbg["fbf"] = dd("dbg_fbf", [128, NT, C], BF16)

    x_t = x_d.rearrange("(t p) c -> t p c", p=128)
    out_t = out_d.rearrange("(t p) c -> t p c", p=128)

    sbr = lambda n, s, d: ctx.enter_context(nc.sbuf_tensor(n, s, d, side="right"))
    ident = sbr("ident", [128, 128], BF16)
    trineg = sbr("trineg", [128, 128], F32)
    ones1 = sbr("ones1", [1, 128], BF16)
    bprow_bf = sbr("bprow_bf", [1, C], BF16)
    fbrow_bf = sbr("fbrow_bf", [1, C], BF16)
    bias_bc = sbr("bias_bc", [128, C], F32)
    stA = [sbr(f"stA{i}", [128, 12], F32) for i in range(2)]
    stA3 = sbr("stA3", [128, 12], F32)
    stB = [sbr(f"stB{i}", [128, 2], F32) for i in range(2)]
    sd_t = [sbr(f"sd{i}", [128, 1], F32) for i in range(2)]
    rin_t = [sbr(f"rin{i}", [128, 1], F32) for i in range(2)]
    stB_all = sbr("stB_all", [128, 2, NT], F32)
    sd_all = sbr("sd_all", [128, NT], F32)
    rin_all = sbr("rin_all", [128, NT], F32)
    lnl_bf = [sbr(f"lnl{i}", [1, 512], BF16) for i in range(2)]
    lbc_sb = [sbr(f"lbc{i}", [64, 512], BF16) for i in range(2)]
    oTun = [sbr(f"oTun{i}", [65, 512], BF16) for i in range(2)]
    eps_t = sbr("eps_t", [128, 1], F32)
    qscale_t = sbr("qscale_t", [128, 1], F32)

    KB = 1024
    arA = ctx.enter_context(nc.sbuf_tensor("arA", [128, 28 * KB], BF16)).ap()
    arB = ctx.enter_context(nc.sbuf_tensor("arB", [128, 16 * KB], BF16)).ap()
    arC = ctx.enter_context(nc.sbuf_tensor("arC", [128, 16 * KB + 256], BF16)).ap()
    arD = ctx.enter_context(nc.sbuf_tensor("arD", [128, 16 * KB], BF16)).ap()
    arE = ctx.enter_context(nc.sbuf_tensor("arE", [128, 16 * KB], BF16)).ap()

    wqkvT = _carve(arA, 0, BF16, [NC, 3 * C])
    xa = [_carve(arD, i * 16 * KB, F32, [4, C]) for i in range(2)]
    x1bf = [_carve(arE, i * 2 * KB, BF16, [C]) for i in range(2)]
    pTt = [_carve(arA, 48 * KB + i * 2 * KB, BF16, [1024]) for i in range(3)]
    x3bf = [_carve(arA, 48 * KB + i * 2 * KB, BF16, [C]) for i in range(2)]
    wprojT = _carve(arA, 0, BF16, [NC, C])
    W1sb = _carve(arA, 16 * KB, BF16, [NC, C])
    W2sb = _carve(arA, 32 * KB, BF16, [NC, C])
    W3sb = _carve(arA, 0, BF16, [NC, C])
    xa2 = [_carve(arA, 48 * KB + i * 4 * KB, F32, [C]) for i in range(2)]
    h3sb = [_carve(arA, 16 * KB + i * 4 * KB, F32, [C]) for i in range(2)]
    fsb = [_carve(arA, 24 * KB + i * 4 * KB, F32, [C]) for i in range(2)]
    lnf = [_carve(arA, 48 * KB + i * 4 * KB, F32, [C]) for i in range(2)]
    outt = [_carve(arA, 32 * KB + i * 8 * KB, F32, [2, C]) for i in range(2)]

    x1T = _carve(arB, 0, BF16, [NC, T])
    x2b = _carve(arB, 0, F32, [NT // 2, C])

    v65 = _carve(arC, 0, BF16, [NT, H * 65])
    o2sb = [_carve(arC, i * 4 * KB, F32, [C]) for i in range(2)]
    lnout = [_carve(arC, 8 * KB + i * 4 * KB, F32, [C]) for i in range(2)]
    AB1 = _carve(arC, 0, BF16, [NC, T])
    f_bf = _carve(arC, 0, BF16, [NT, C])

    qkT = [_carve(arD, j * 8 * KB, BF16, [2, T]) for j in range(2)]
    x2a = _carve(arD, 0, F32, [NT // 2, C])
    oT = _carve(arE, 0, BF16, [NC, T])
    AB0 = _carve(arE, 0, BF16, [NC, T])

    def x2v(t):
        return x2a[:, t, :] if t < NT // 2 else x2b[:, t - NT // 2, :]

    v65v = v65.rearrange("p t (h e) -> p t h e", h=H)
    wqkvT3 = wqkvT.rearrange("p c (w r) -> p c w r", w=3)

    sems = {}
    for name in ["sLx0", "sLx1", "sLW", "sLW2", "sLP", "sG", "sGB", "sGQ", "sGQ2",
                 "sGPc", "sGW", "sP", "sV", "sS", "sO0", "sO1", "sDG"]:
        sems[name] = Sem(ctx, nc, name)

    guards = {}
    alloc_cache = {}
    walk_state = {"first": True}

    def ps_open(name, shape, dtype):
        if walk_state["first"]:
            g = nc.psum_tensor(name, shape, dtype)
            tt = g.__enter__()
            guards[name] = g
            alloc_cache[name] = tt
        return alloc_cache[name]

    def ps_close(*names):
        if walk_state["first"]:
            for name in names:
                guards.pop(name).__exit__(None, None, None)

    blk = ctx.enter_context(nc.Block())

    def walk(p: P):
        def dump(key, src_ap, wait_sem=None, wait_val=None, after_inc=None):
            if not debug:
                return
            if wait_sem is not None:
                p.wait("gpsimd", wait_sem, wait_val)
            p.do("gpsimd", lambda g: g.dma_start(out=dbg[key], in_=src_ap),
                 inc=[(p.sems["sDG"], 16)])
            p.wait("gpsimd", p.sems["sDG"])
            if after_inc is not None:
                p.do("gpsimd", lambda g: g.engine_nop(), inc=[(after_inc, 1)])
        (sLx0, sLx1, sLW, sLW2, sLP, sG, sGB, sGQ, sGQ2, sGPc, sGW, sP, sV, sS,
         sO0, sO1) = (
            p.sems[k] for k in
            ["sLx0", "sLx1", "sLW", "sLW2", "sLP", "sG", "sGB", "sGQ", "sGQ2",
             "sGPc", "sGW", "sP", "sV", "sS", "sO0", "sO1"])
        sLx = [sLx0, sLx1]
        sO = [sO0, sO1]

        p.do("gpsimd", lambda g: g.memset(ident[:, :], 0.0))
        p.do("gpsimd", lambda g: g.memset(trineg[:, :], 0.0))
        p.do("gpsimd", lambda g: g.memset(ones1[:, :], 1.0))
        p.do("gpsimd", lambda g: g.memset(eps_t[:, :], EPS))
        p.do("gpsimd", lambda g: g.memset(qscale_t[:, :], 0.125), inc=[(sG, 1)])
        p.wait("gpsimd", sG)
        p.do("gpsimd", lambda g: g.affine_select(
            out=ident[:, :], in_=ident[:, :], compare_op=ALU.not_equal, fill=1.0,
            base=0, pattern=[[-1, 128]], channel_multiplier=1))
        p.do("gpsimd", lambda g: g.affine_select(
            out=trineg[:, :], in_=trineg[:, :], compare_op=ALU.is_ge, fill=-1e9,
            base=0, pattern=[[1, 128]], channel_multiplier=-1), inc=[(sG, 1)])
        const_done = sG.n
        p.do("gpsimd", lambda g: g.dma_start(out=bprow_bf[:, :], in_=bproj_d),
             inc=[(sGB, 16)])
        p.do("gpsimd", lambda g: g.dma_start(out=fbrow_bf[:, :], in_=fbias_d),
             inc=[(sGB, 16)])
        bias_bf_done = sGB.n

        p.wait("scalar", sG, const_done)
        p.wait("vector", sG, const_done)

        pbias = ps_open("pbias", [128, C], F32)
        p.wait("tensor", sGB, bias_bf_done)
        p.do("tensor", lambda t: t.matmul(pbias[:, 0:512], ones1[:, :],
                                          bprow_bf[:, 0:512], start=True, stop=True))
        p.do("tensor", lambda t: t.matmul(pbias[:, 512:1024], ones1[:, :],
                                          bprow_bf[:, 512:1024], start=True, stop=True),
             inc=[(sP, 1)])
        p.wait("vector", sP)
        p.do("vector", lambda v: v.tensor_copy(bias_bc[:, :], pbias[:, :]), inc=[(sV, 1)])
        bpbc_done = sV.n
        ps_close("pbias")

        if upto < 10:
            return
        ptr = [ps_open(f"ptr{i}", [128, C], BF16) for i in range(2)]
        xa_free = [None] * 2
        x1bf_free = [None] * 2
        ptr_free = [None] * 2
        stats_free = [None] * 2
        wqkvT_done = 0
        def p1_load2(tp4):
            b = tp4 % 2
            if xa_free[b] is not None:
                p.wait("sync", *xa_free[b])
            src = x_d[tp4 * 512:(tp4 + 1) * 512, :].rearrange("(u p) c -> p u c", p=128)
            nsp = 4
            uw = 4 // nsp
            for i in range(nsp):
                p.do("sync", lambda s, b=b, src=src, i=i, uw=uw: s.dma_start(
                    out=xa[b][:, i * uw:(i + 1) * uw, :],
                    in_=src[:, i * uw:(i + 1) * uw, :]), inc=[(sLx[b], 16)])
            return (sLx[b], sLx[b].n)

        def p1_stats(t, x_loaded):
            sb = t % 2
            u = t % 4
            b = (t // 4) % 2
            p.wait("vector", *x_loaded)
            if stats_free[sb] is not None:
                p.wait("vector", *stats_free[sb])
            p.do("vector", lambda v, b=b, u=u, sb=sb: v.bn_stats(stA[sb][:, 0:6], xa[b][:, u, 0:512]))
            p.do("vector", lambda v, b=b, u=u, sb=sb: v.bn_stats(stA[sb][:, 6:12], xa[b][:, u, 512:1024]),
                 inc=[(sV, 1)])
            p.wait("vector", sV)
            p.do("vector", lambda v, sb=sb: v.bn_aggr(stB[sb][:, :], stA[sb][:, :]), inc=[(sV, 1)])
            aggr_done = sV.n
            p.wait("scalar", sV, aggr_done)
            p.do("scalar", lambda s, sb=sb: s.activation(sd_t[sb][:, :], stB[sb][:, 1:2],
                                                         AF.Sqrt, bias=eps_t[:, :]), inc=[(sS, 1)])
            return sS.n

        def p1_apply(t, sq_done):
            sb = t % 2
            u = t % 4
            b = (t // 4) % 2
            p.wait("vector", sS, sq_done)
            p.do("vector", lambda v, sb=sb: v.reciprocal(rin_t[sb][:, :], sd_t[sb][:, :]),
                 inc=[(sV, 1)])
            p.wait("vector", sV)
            if x1bf_free[sb] is not None:
                p.wait("vector", *x1bf_free[sb])
            p.do("vector", lambda v, b=b, u=u, sb=sb: v.tensor_scalar(
                x1bf[sb][:, :], xa[b][:, u, :], stB[sb][:, 0:1], rin_t[sb][:, :],
                ALU.subtract, ALU.mult), inc=[(sV, 1)])
            apply_done = sV.n
            if t % 4 == 3:
                xa_free[b] = (sV, apply_done)
            stats_free[sb] = (sV, apply_done)
            p.wait("tensor", sV, apply_done)
            p.wait("tensor", sG, const_done)
            if ptr_free[sb] is not None:
                p.wait("tensor", *ptr_free[sb])
            for cc in range(NC):
                p.do("tensor", lambda te, sb=sb, cc=cc: te.transpose(
                    ptr[sb][:, cc * 128:(cc + 1) * 128],
                    x1bf[sb][:, cc * 128:(cc + 1) * 128], ident[:, :]),
                    inc=[(sP, 1)] if cc == NC - 1 else None)
            tr_done = sP.n
            x1bf_free[sb] = (sP, tr_done)
            p.wait("scalar", sP, tr_done)
            p.do("scalar", lambda s, sb=sb, t=t: s.copy(
                x1T[:, :, t * 128:(t + 1) * 128],
                ptr[sb].rearrange("p (c q) -> p c q", c=NC)), inc=[(sS, 1)])
            ptr_free[sb] = (sS, sS.n)

        ld_rec = {}
        sq_rec = {}
        rec = {}
        ld_rec[0] = p1_load2(0)
        ld_rec[1] = p1_load2(1)
        sq_rec[0] = p1_stats(0, ld_rec[0])
        for hg in range(4):
            p.do("gpsimd", lambda g, hg=hg: g.dma_start(
                out=wqkv_bf_d[2 * C + hg * 256:2 * C + (hg + 1) * 256, :]
                    .rearrange("(h d) c -> h d c", h=4),
                in_=wqkv_d[hg * 4:(hg + 1) * 4, 2 * HS:3 * HS, :]), inc=[(sGQ2, 16)])
        wqkv2_cast_done = sGQ2.n
        p.do("gpsimd", lambda g: g.memset(v65v[:, :, :, 64:65], 1.0), inc=[(sG, 1)])
        v65ones_done = sG.n

        p.wait("sync", sGQ2, wqkv2_cast_done)
        for cc in range(NC):
            p.do("sync", lambda s, cc=cc: s.dma_start_transpose(
                out=wqkvT3[:, cc, 2, :],
                in_=wqkv_bf_d[2 * C:3 * C, cc * 128:(cc + 1) * 128]),
                inc=[(sLW2, 16)])
        rec["wqkvT2_done"] = sLW2.n

        p.wait("gpsimd", sLW2, rec["wqkvT2_done"])
        for w in (0, 1):
            for hh in range(2):
                p.do("gpsimd", lambda g, w=w, hh=hh: g.dma_start(
                    out=wqkv_bf_d[w * C + hh * 512:w * C + (hh + 1) * 512, :]
                        .rearrange("(h d) c -> h d c", h=8),
                    in_=wqkv_d[hh * 8:(hh + 1) * 8, w * HS:(w + 1) * HS, :]),
                    inc=[(sGQ, 16)])
        wqkv_cast_done = sGQ.n

        pv = [ps_open(f"pv{i}", [128, C], F32) for i in range(2)]
        pv_free = [None] * 2
        x1T_tile = {}
        VSKEW = 6

        def emit_v(t):
            pb = t % 2
            if pv_free[pb] is not None:
                p.wait("tensor", *pv_free[pb])
            p.wait("tensor", sS, x1T_tile[t])
            p.wait("tensor", sLW2, rec["wqkvT2_done"])
            for half in range(2):
                for cc in range(NC):
                    p.do("tensor", lambda te, t=t, pb=pb, half=half, cc=cc: te.matmul(
                        pv[pb][:, half * 512:(half + 1) * 512],
                        x1T[:, cc, t * 128:(t + 1) * 128],
                        wqkvT3[:, cc, 2, half * 512:(half + 1) * 512],
                        start=(cc == 0), stop=(cc == NC - 1)),
                        inc=[(sP, 1)] if (cc == NC - 1 and half == 1) else None)
            v_mm_done = sP.n
            p.wait("scalar", sP, v_mm_done)
            p.do("scalar", lambda s, t=t, pb=pb: s.copy(
                v65v[:, t, :, 0:64],
                pv[pb].rearrange("p (h e) -> p h e", h=H)), inc=[(sS, 1)])
            pv_free[pb] = (sS, sS.n)

        for t in range(NT):
            g = t // 4 + 1
            if t % 4 == 0 and 2 <= g < NT // 4:
                ld_rec[g] = p1_load2(g)
            if t + 1 < NT:
                sq_rec[t + 1] = p1_stats(t + 1, ld_rec[(t + 1) // 4])
            p1_apply(t, sq_rec[t])
            x1T_tile[t] = sS.n
            if t >= VSKEW:
                emit_v(t - VSKEW)
        for t in range(NT - VSKEW, NT):
            emit_v(t)
        x1T_done_S = x1T_tile[NT - 1]
        v_done_S = sS.n

        p.wait("sync", sGQ, wqkv_cast_done)
        for w in range(2):
            for cc in range(NC):
                p.do("sync", lambda s, w=w, cc=cc: s.dma_start_transpose(
                    out=wqkvT3[:, cc, w, :],
                    in_=wqkv_bf_d[w * C:(w + 1) * C, cc * 128:(cc + 1) * 128]),
                    inc=[(sLW, 16)])
        rec["wqkvT_done"] = sLW.n
        ps_close("pv1", "pv0", "ptr1", "ptr0")

        if upto < 20:
            return
        dump("x1T", x1T[:, :, :], sS, x1T_done_S)
        p.wait("gpsimd", sS, x1T_done_S)
        p.do("gpsimd", lambda g: g.dma_start(out=wproj_bf_d, in_=wproj_d),
             inc=[(sGPc, 16)])
        wproj_cast_done = sGPc.n

        if upto < 30:
            return
        pqk = [ps_open(f"pqk{i}", [128, 512], F32) for i in range(2)]
        pqk_free = [None] * 2
        p.wait("tensor", sS, v_done_S)
        filler_done = {}
        qk_rec = {}
        rec2 = {}

        def qk_units(pnext):
            slot = pnext % 2
            for which in range(2):
                for tq in range(NQC):
                    alt = (which * NQC + tq) % 2
                    for cc in range(NC):
                        def mm(cc=cc, which=which, tq=tq, alt=alt, pnext=pnext):
                            if cc == 0 and pqk_free[alt] is not None:
                                p.wait("tensor", *pqk_free[alt])
                            p.do("tensor", lambda te, cc=cc, which=which, tq=tq, alt=alt, pnext=pnext: te.matmul(
                                pqk[alt][:, :],
                                wqkvT3[:, cc, which, pnext * 128:(pnext + 1) * 128],
                                x1T[:, cc, tq * 512:(tq + 1) * 512],
                                start=(cc == 0), stop=(cc == NC - 1)),
                                inc=[(sP, 1)] if cc == NC - 1 else None)
                            if cc == NC - 1:
                                qk_rec[(pnext, which, tq)] = sP.n
                        yield mm

                    def evac(which=which, tq=tq, alt=alt, slot=slot, pnext=pnext):
                        p.wait("vector", sP, qk_rec[(pnext, which, tq)])
                        if which == 0:
                            p.do("vector", lambda v, alt=alt, slot=slot, tq=tq: v.tensor_scalar_mul(
                                qkT[slot][:, 0, tq * 512:(tq + 1) * 512],
                                pqk[alt][:, :], qscale_t[:, :]), inc=[(sV, 1)])
                        else:
                            p.do("vector", lambda v, alt=alt, slot=slot, tq=tq: v.tensor_copy(
                                qkT[slot][:, 1, tq * 512:(tq + 1) * 512],
                                pqk[alt][:, :]), inc=[(sV, 1)])
                        pqk_free[alt] = (sV, sV.n)
                    yield evac

        p.wait("tensor", sLW, rec["wqkvT_done"])
        for u in qk_units(0):
            u()
        filler_done[0] = (sV, sV.n)

        NPAR = 2
        ps_s = [ps_open(f"pss{i}", [128, 1024], F32) for i in range(NPAR)]
        ps_o = [ps_open(f"po{h}", [128, 512], F32) for h in range(2)]
        pT = pTt
        ps_s_free = [None] * NPAR
        NPT = 3
        pT_free = [None] * NPT
        ps_o_free = [None, None]
        lbc_free = [None, None]
        lnl_free = [None, None]
        oTun_free = [None, None]
        tail_stash = [None]

        p.wait("tensor", sG, v65ones_done)

        tail_q = []
        ucnt = [0]
        gkc_c = [0]
        cur_fil = [iter(())]

        def bcast_unit(h, lnl_val, pp, qc, alt):
            p.wait("tensor", sS, lnl_val)
            if lbc_free[h] is not None:
                p.wait("tensor", *lbc_free[h])
            if pqk_free[alt] is not None:
                p.wait("tensor", *pqk_free[alt])
            p.do("tensor", lambda te, h=h, alt=alt: te.matmul(
                pqk[alt][0:64, :], ones1[:, 0:64], lnl_bf[h][:, :],
                start=True, stop=True), inc=[(sP, 1)])
            lbc_mm = sP.n
            lnl_free[h] = (sP, lbc_mm)
            p.wait("scalar", sP, lbc_mm)
            p.do("scalar", lambda s, h=h, alt=alt: s.activation(
                lbc_sb[h][:, :], pqk[alt][0:64, :], AF.Exp, scale=-1.0), inc=[(sS, 1)])
            lbcexp_done = sS.n
            pqk_free[alt] = (sS, lbcexp_done)
            lo = h * 64
            p.wait("vector", sS, lbcexp_done)
            p.do("vector", lambda v, h=h, pp=pp, qc=qc, lo=lo: v.tensor_tensor(
                oT[lo:lo + 64, pp, qc * 512:(qc + 1) * 512],
                oTun[h][0:64, :], lbc_sb[h][:, :], ALU.mult), inc=[(sV, 1)])
            done = sV.n
            oTun_free[h] = (sV, done)
            lbc_free[h] = (sV, done)

        def pop_bcast():
            tail_q.pop(0)(((ucnt[0] // 9) + 1) % 2)

        def flush_tail_q():
            while tail_q:
                if ucnt[0] % 9 == 0:
                    pop_bcast()
                    continue
                u = next(cur_fil[0], None)
                if u is None:
                    pop_bcast()
                else:
                    u()
                    ucnt[0] += 1

        def emit_tail():
            if tail_stash[0] is None:
                return
            flush_tail_q()
            pp, qc, ov_get = tail_stash[0]
            ov_done = ov_get()
            tail_stash[0] = None
            for h in range(2):
                po = ps_o[h]
                p.wait("vector", sP, ov_done)
                if oTun_free[h] is not None:
                    p.wait("vector", *oTun_free[h])
                p.do("vector", lambda v, h=h, po=po: v.tensor_copy(oTun[h][:, :], po[0:65, :]),
                     inc=[(sV, 1)])
                otun_done = sV.n
                ps_o_free[h] = (sV, otun_done)
                p.wait("scalar", sV, otun_done)
                if lnl_free[h] is not None:
                    p.wait("scalar", *lnl_free[h])
                p.do("scalar", lambda s, h=h: s.activation(
                    lnl_bf[h][:, :], oTun[h][64:65, :], AF.Ln), inc=[(sS, 1)])
                lnl_done = sS.n
                tail_q.append(lambda alt, h=h, lnl_done=lnl_done, pp=pp, qc=qc: bcast_unit(
                    h, lnl_done, pp, qc, alt))

        for pp in range(8):
            slot = pp % 2
            cur_fil[0] = qk_units(pp + 1) if pp < 7 else iter(())

            def fill(n):
                for _ in range(n):
                    if ucnt[0] % 9 == 0 and tail_q:
                        pop_bcast()
                    u = next(cur_fil[0], None)
                    if u is None:
                        return
                    u()
                    ucnt[0] += 1

            p.wait("tensor", *filler_done[pp])
            exp_rec = {}
            pend = []
            last_av_P = {}

            def emit_av(ent, pp=pp, exp_rec=exp_rec, last_av_P=last_av_P):
                qc_, nkc_, kc, par = ent
                sem, val = exp_rec[(qc_, kc)]
                p.wait("tensor", sem, val)
                for h in range(2):
                    if kc == 0 and ps_o_free[h] is not None:
                        p.wait("tensor", *ps_o_free[h])
                    p.do("tensor", lambda te, pp=pp, kc=kc, h=h, par=par, nkc_=nkc_: te.matmul(
                        ps_o[h][0:65, :],
                        v65v[:, kc, 2 * pp + h, :],
                        pT[par][:, h * 512:(h + 1) * 512],
                        start=(kc == 0), stop=(kc == nkc_ - 1)), inc=[(sP, 1)])
                pT_free[par] = (sP, sP.n)
                last_av_P[qc_] = sP.n

            for qc in range(NQC):
                nkc = 4 * qc + 4

                for kc in range(nkc):
                    par = kc % NPAR
                    parp = gkc_c[0] % NPT
                    gkc_c[0] += 1
                    diag = kc >= 4 * qc
                    dq = 128 * (kc - 4 * qc) if diag else 0
                    if ps_s_free[par] is not None:
                        p.wait("tensor", *ps_s_free[par])
                    for h in range(2):
                        lo = h * 64
                        p.do("tensor", lambda te, pp=pp, qc=qc, kc=kc, h=h, par=par, lo=lo, slot=slot: te.matmul(
                            ps_s[par][:, h * 512:(h + 1) * 512],
                            qkT[slot][lo:lo + 64, 1, kc * 128:(kc + 1) * 128],
                            qkT[slot][lo:lo + 64, 0, qc * 512:(qc + 1) * 512],
                            start=True, stop=True),
                            inc=[(sP, 1)] if h == 1 else None)
                    s_done = sP.n
                    if len(pend) > 2:
                        emit_av(pend.pop(0))
                    if kc == 2:
                        emit_tail()
                    fill(2)
                    if diag:
                        p.wait("vector", sP, s_done)
                        for h in range(2):
                            p.do("vector", lambda v, h=h, par=par, dq=dq: v.tensor_tensor(
                                ps_s[par][:, h * 512 + dq:h * 512 + dq + 128],
                                ps_s[par][:, h * 512 + dq:h * 512 + dq + 128],
                                trineg[:, :], ALU.add),
                                inc=[(sV, 1)] if h == 1 else None)
                        smask_done = sV.n
                    if diag:
                        p.wait("scalar", sV, smask_done)
                    else:
                        p.wait("scalar", sP, s_done)
                    if pT_free[parp] is not None:
                        p.wait("scalar", *pT_free[parp])
                    p.do("scalar", lambda s, par=par, parp=parp, dq=dq: s.activation(
                        pT[parp][:, dq:1024], ps_s[par][:, dq:1024], AF.Exp), inc=[(sS, 1)])
                    ps_s_free[par] = (sS, sS.n)
                    gate = (sS, sS.n)
                    if dq > 0:
                        p.wait("gpsimd", sS, gate[1])
                        p.do("gpsimd", lambda g, parp=parp, dq=dq: g.memset(
                            pT[parp][:, 0:dq], 0.0))
                        p.do("gpsimd", lambda g, parp=parp, dq=dq: g.memset(
                            pT[parp][:, 512:512 + dq], 0.0), inc=[(sG, 1)])
                        gate = (sG, sG.n)
                    exp_rec[(qc, kc)] = gate
                    pend.append((qc, nkc, kc, parp))
                    fill(1)
                tail_stash[0] = (pp, qc, (lambda d=last_av_P, q=qc: d[q]))
                if qc == 2:
                    fill(10 ** 6)
            fill(10 ** 6)
            for ent in pend:
                emit_av(ent)
            pend.clear()
            if pp < 7:
                filler_done[pp + 1] = (sV, sV.n)
            if pp == 6:
                rec2["fil7_P"] = sP.n
        emit_tail()
        flush_tail_q()
        attn_done_V = sV.n
        ps_close("po1", "po0", "pss1", "pss0", "pqk1", "pqk0")

        p.wait("sync", sGPc, wproj_cast_done)
        p.wait("sync", sP, rec2["fil7_P"])
        for cc in range(NC):
            p.do("sync", lambda s, cc=cc: s.dma_start_transpose(
                out=wprojT[:, cc, :], in_=wproj_bf_d[:, cc * 128:(cc + 1) * 128]),
                inc=[(sLP, 16)])
        wprojT_done = sLP.n
        p.wait("gpsimd", sP, rec2["fil7_P"])
        p.do("gpsimd", lambda g: g.dma_start(
            out=W1sb[:, :, :], in_=w1_d.rearrange("(cc p) f -> p cc f", p=128)),
            inc=[(sGW, 16)])
        p.do("gpsimd", lambda g: g.dma_start(
            out=W2sb[:, :, :], in_=w2_d.rearrange("(cc p) f -> p cc f", p=128)),
            inc=[(sGW, 16)])
        W1_done = sGW.n
        W2_done = sGW.n

        if upto < 50:
            return
        oT_dumped = Sem.__new__(Sem); oT_dumped = None
        if debug:
            dump("oT", oT[:, :, :], sV, attn_done_V, after_inc=sG)
            oT_dump_done = sG.n
        po2 = [ps_open(f"po2{i}", [128, C], F32) for i in range(2)]
        po2_free = [None] * 2
        o2_free = [None] * 2
        ln_free = [None] * 2
        stats_free = [None] * 2
        xa2_free = [None] * 2
        p.wait("sync", sV, attn_done_V)
        p.wait("tensor", sV, attn_done_V)
        p.wait("tensor", sLP, wprojT_done)
        p.wait("tensor", sV, bpbc_done)

        def p3_front(t):
            pb = t % 2
            if xa2_free[pb] is not None:
                p.wait("sync", *xa2_free[pb])
            p.do("sync", lambda s, t=t, pb=pb: s.dma_start(out=xa2[pb][:, :], in_=x_t[t]),
                 inc=[(sLx[pb], 16)])
            x2_loaded = (sLx[pb], sLx[pb].n)
            if po2_free[pb] is not None:
                p.wait("tensor", *po2_free[pb])
            for cc in range(NC):
                for fc in range(2):
                    p.do("tensor", lambda te, t=t, pb=pb, cc=cc, fc=fc: te.matmul(
                        po2[pb][:, fc * 512:(fc + 1) * 512],
                        oT[:, cc, t * 128:(t + 1) * 128],
                        wprojT[:, cc, fc * 512:(fc + 1) * 512],
                        start=(cc == 0), stop=(cc == NC - 1)),
                        inc=[(sP, 1)] if (cc == NC - 1 and fc == 1) else None)
            mm_done = sP.n
            p.wait("vector", sP, mm_done)
            if o2_free[pb] is not None:
                p.wait("vector", *o2_free[pb])
            p.do("vector", lambda v, pb=pb: v.tensor_tensor(
                o2sb[pb][:, :], po2[pb][:, :], bias_bc[:, :], ALU.add), inc=[(sV, 1)])
            po2_free[pb] = (sV, sV.n)
            p.wait("vector", sV)
            if stats_free[pb] is not None:
                p.wait("vector", *stats_free[pb])
            p.do("vector", lambda v, pb=pb: v.bn_stats(stA[pb][:, 0:6], o2sb[pb][:, 0:512]))
            p.do("vector", lambda v, pb=pb: v.bn_stats(stA[pb][:, 6:12], o2sb[pb][:, 512:1024]),
                 inc=[(sV, 1)])
            p.wait("vector", sV)
            p.do("vector", lambda v, pb=pb: v.bn_aggr(stB[pb][:, :], stA[pb][:, :]), inc=[(sV, 1)])
            aggr_done = sV.n
            p.wait("scalar", sV, aggr_done)
            p.do("scalar", lambda s, pb=pb: s.activation(sd_t[pb][:, :], stB[pb][:, 1:2],
                                                         AF.Sqrt, bias=eps_t[:, :]), inc=[(sS, 1)])
            return (sS.n, x2_loaded)

        def p3_back(t, rec):
            pb = t % 2
            sq_done, x2_loaded = rec
            p.wait("vector", sS, sq_done)
            p.do("vector", lambda v, pb=pb: v.reciprocal(rin_t[pb][:, :], sd_t[pb][:, :]),
                 inc=[(sV, 1)])
            p.wait("vector", sV)
            if ln_free[pb] is not None:
                p.wait("vector", *ln_free[pb])
            p.do("vector", lambda v, pb=pb: v.tensor_scalar(
                lnout[pb][:, :], o2sb[pb][:, :], stB[pb][:, 0:1], rin_t[pb][:, :],
                ALU.subtract, ALU.mult), inc=[(sV, 1)])
            o2_free[pb] = (sV, sV.n)
            stats_free[pb] = (sV, sV.n)
            lnout_done = sV.n
            p.wait("gpsimd", sV, lnout_done)
            p.wait("gpsimd", *x2_loaded)
            p.do("gpsimd", lambda g, pb=pb, t=t: g.tensor_tensor(
                x2v(t), lnout[pb][:, :], xa2[pb][:, :], ALU.add), inc=[(sG, 1)])
            x2add[t] = sG.n
            ln_free[pb] = (sG, sG.n)
            xa2_free[pb] = (sG, sG.n)

        x2add = {}
        ln3_sq = {}

        def ln3_stats(t):
            p.wait("vector", sG, x2add[t])
            p.wait("vector", sV)
            p.do("vector", lambda v, t=t: v.bn_stats(stA3[:, 0:6], x2v(t)[:, 0:512]))
            p.do("vector", lambda v, t=t: v.bn_stats(stA3[:, 6:12], x2v(t)[:, 512:1024]),
                 inc=[(sV, 1)])
            p.wait("vector", sV)
            p.do("vector", lambda v, t=t: v.bn_aggr(stB_all[:, :, t], stA3[:, :]),
                 inc=[(sV, 1)])
            ag = sV.n
            p.wait("scalar", sV, ag)
            p.do("scalar", lambda s, t=t: s.activation(
                sd_all[:, t:t + 1], stB_all[:, 1, t:t + 1],
                AF.Sqrt, bias=eps_t[:, :]), inc=[(sS, 1)])
            ln3_sq[t] = sS.n

        p3_rec = {0: p3_front(0)}
        for t in range(NT):
            if t + 1 < NT:
                p3_rec[t + 1] = p3_front(t + 1)
            p3_back(t, p3_rec[t])
            if t >= 1:
                ln3_stats(t - 1)
        ln3_stats(NT - 1)
        x2_done_V = sV.n
        x2_done_G = sG.n
        ps_close("po21", "po20")

        if upto < 60:
            return
        if debug:
            dump("x2a", x2a[:, :, :], sG, x2_done_G)
            dump("x2b", x2b[:, :, :], sG, x2_done_G)
        ptrb = [ps_open(f"ptrb{i}", [128, C], BF16) for i in range(2)]
        ptr_free = [None] * 2
        x3bf_free = [None] * 2
        p.wait("vector", sV, x2_done_V)
        p.wait("vector", sG, x2_done_G)

        def ln3_back(t):
            sb = t % 2
            p.wait("vector", sS, ln3_sq[t])
            p.do("vector", lambda v, sb=sb, t=t: v.reciprocal(
                rin_t[sb][:, :], sd_all[:, t:t + 1]), inc=[(sV, 1)])
            p.wait("vector", sV)
            if x3bf_free[sb] is not None:
                p.wait("vector", *x3bf_free[sb])
            p.do("vector", lambda v, t=t, sb=sb: v.tensor_scalar(
                x3bf[sb][:, :], x2v(t), stB_all[:, 0, t:t + 1], rin_t[sb][:, :],
                ALU.subtract, ALU.mult), inc=[(sV, 1)])
            apply_done = sV.n
            p.wait("tensor", sV, apply_done)
            if ptr_free[sb] is not None:
                p.wait("tensor", *ptr_free[sb])
            for cc in range(NC):
                p.do("tensor", lambda te, sb=sb, cc=cc: te.transpose(
                    ptrb[sb][:, cc * 128:(cc + 1) * 128],
                    x3bf[sb][:, cc * 128:(cc + 1) * 128], ident[:, :]),
                    inc=[(sP, 1)] if cc == NC - 1 else None)
            tr_done = sP.n
            x3bf_free[sb] = (sP, tr_done)
            p.wait("scalar", sP, tr_done)
            p.do("scalar", lambda s, sb=sb, t=t: s.copy(
                AB0[:, :, t * 128:(t + 1) * 128],
                ptrb[sb].rearrange("p (c q) -> p c q", c=NC)), inc=[(sS, 1)])
            ptr_free[sb] = (sS, sS.n)

        for t in range(NT):
            ln3_back(t)
        x3T_done_S = sS.n
        ps_close("ptrb1", "ptrb0")

        if upto < 70:
            return
        pf = [ps_open(f"pf{i}", [128, 512], F32) for i in range(4)]
        pf_free = [None] * 4

        def ffn_fm(Wsb, srcT, dstT, w_sem, w_val, s_sem, s_val):
            p.wait("tensor", w_sem, w_val)
            p.wait("tensor", s_sem, s_val)
            for fr in range(NC):
                for cc in range(NC):
                    for tq in range(NQC):
                        if cc == 0 and pf_free[tq] is not None:
                            p.wait("tensor", *pf_free[tq])
                        p.do("tensor", lambda te, fr=fr, cc=cc, tq=tq: te.matmul(
                            pf[tq][:, :],
                            Wsb[:, cc, fr * 128:(fr + 1) * 128],
                            srcT[:, cc, tq * 512:(tq + 1) * 512],
                            start=(cc == 0), stop=(cc == NC - 1)),
                            inc=[(sP, 1)] if cc == NC - 1 else None)
                mm_done = sP.n
                for tq in range(NQC):
                    p.wait("scalar", sP, mm_done - (NQC - 1 - tq))
                    p.do("scalar", lambda s, fr=fr, tq=tq: s.copy(
                        dstT[:, fr, tq * 512:(tq + 1) * 512], pf[tq][:, :]), inc=[(sS, 1)])
                    pf_free[tq] = (sS, sS.n)
            return sS.n

        h1_done_S = ffn_fm(W1sb, AB0, AB1, sGW, W1_done, sS, x3T_done_S)
        p.wait("gpsimd", sS, h1_done_S)
        p.do("gpsimd", lambda g: g.dma_start(
            out=W3sb[:, :, :], in_=w3_d.rearrange("(cc p) f -> p cc f", p=128)),
            inc=[(sGW, 16)])
        W3_done = sGW.n
        h2_done_S = ffn_fm(W2sb, AB1, AB0, sGW, W2_done, sS, h1_done_S)
        ps_close("pf3", "pf2", "pf1", "pf0")

        pbias2 = ps_open("pbias2", [128, C], F32)
        p.wait("tensor", sS, h2_done_S)
        p.wait("tensor", sV, x2_done_V)
        p.do("tensor", lambda t: t.matmul(pbias2[:, 0:512], ones1[:, :],
                                          fbrow_bf[:, 0:512], start=True, stop=True))
        p.do("tensor", lambda t: t.matmul(pbias2[:, 512:1024], ones1[:, :],
                                          fbrow_bf[:, 512:1024], start=True, stop=True),
             inc=[(sP, 1)])
        p.wait("vector", sP)
        p.do("vector", lambda v: v.tensor_copy(bias_bc[:, :], pbias2[:, :]), inc=[(sV, 1)])
        fbbc_done = sV.n
        ps_close("pbias2")

        if upto < 80:
            return
        ph3 = [ps_open(f"ph3{i}", [128, C], F32) for i in range(2)]
        ph3_free = [None] * 2
        h3sb_free = [None] * 2
        lnf_free = [None] * 2
        outt_free = [None] * 2
        rstd_grp = {}
        aggr4 = {}
        LAG4 = 5
        p.wait("tensor", sV, fbbc_done)
        p.wait("tensor", sS, h2_done_S)
        p.wait("tensor", sGW, W3_done)

        def emit_4c(t):
            pb = t % 2
            if ph3_free[pb] is not None:
                p.wait("tensor", *ph3_free[pb])
            for cc in range(NC):
                for fc in range(2):
                    p.do("tensor", lambda te, t=t, pb=pb, cc=cc, fc=fc: te.matmul(
                        ph3[pb][:, fc * 512:(fc + 1) * 512],
                        AB0[:, cc, t * 128:(t + 1) * 128],
                        W3sb[:, cc, fc * 512:(fc + 1) * 512],
                        start=(cc == 0), stop=(cc == NC - 1)),
                        inc=[(sP, 1)] if (cc == NC - 1 and fc == 1) else None)
            mm_done = sP.n
            p.wait("vector", sP, mm_done)
            p.wait("vector", sV, fbbc_done)
            if h3sb_free[pb] is not None:
                p.wait("vector", *h3sb_free[pb])
            p.do("vector", lambda v, pb=pb: v.tensor_tensor(
                h3sb[pb][:, :], ph3[pb][:, :], bias_bc[:, :], ALU.add), inc=[(sV, 1)])
            badd_done = sV.n
            ph3_free[pb] = (sV, badd_done)
            p.wait("scalar", sV, badd_done)
            p.do("scalar", lambda s, pb=pb, t=t: s.activation(
                f_bf[:, t, :], h3sb[pb][:, :], AF.Tanh), inc=[(sS, 1)])
            tanh_done = sS.n
            h3sb_free[pb] = (sS, tanh_done)
            p.wait("vector", sS, tanh_done)
            p.do("vector", lambda v, t=t: v.bn_stats(stA[pb][:, 0:6], f_bf[:, t, 0:512]))
            p.do("vector", lambda v, t=t: v.bn_stats(stA[pb][:, 6:12], f_bf[:, t, 512:1024]),
                 inc=[(sV, 1)])
            p.wait("vector", sV)
            p.do("vector", lambda v, pb=pb, t=t: v.bn_aggr(stB_all[:, :, t], stA[pb][:, :]),
                 inc=[(sV, 1)])
            aggr4[t] = sV.n

        def emit_rstd4(g):
            p.wait("scalar", sV, aggr4[4 * g + 3])
            p.do("scalar", lambda s, g=g: s.activation(
                sd_all[:, 4 * g:4 * g + 4], stB_all[:, 1, 4 * g:4 * g + 4],
                AF.Ln, bias=eps_t[:, :]), inc=[(sS, 1)])
            p.wait("scalar", sS)
            p.do("scalar", lambda s, g=g: s.activation(
                rin_all[:, 4 * g:4 * g + 4], sd_all[:, 4 * g:4 * g + 4],
                AF.Exp, scale=-0.5), inc=[(sS, 1)])
            rstd_grp[g] = sS.n

        def emit_4d(t):
            pb = t % 2
            p.wait("vector", sS, rstd_grp[t // 4])
            if lnf_free[pb] is not None:
                p.wait("vector", *lnf_free[pb])
            p.do("vector", lambda v, pb=pb, t=t: v.tensor_scalar(
                lnf[pb][:, :], f_bf[:, t, :], stB_all[:, 0, t:t + 1], rin_all[:, t:t + 1],
                ALU.subtract, ALU.mult), inc=[(sV, 1)])
            lnf_done = sV.n
            ob = (t // 2) % 2
            u = t % 2
            p.wait("gpsimd", sV, lnf_done)
            if u == 0 and outt_free[ob] is not None:
                p.wait("gpsimd", *outt_free[ob])
            p.do("gpsimd", lambda g, ob=ob, u=u, pb=pb, t=t: g.tensor_tensor(
                outt[ob][:, u, :], lnf[pb][:, :], x2v(t), ALU.add), inc=[(sG, 1)])
            lnf_free[pb] = (sG, sG.n)
            if u == 1:
                tp2 = t // 2
                dst = out_d[tp2 * 256:(tp2 + 1) * 256, :].rearrange("(w p) c -> p w c", p=128)
                p.wait("gpsimd", sG, sG.n)
                p.do("gpsimd", lambda g, ob=ob, dst=dst: g.dma_start(
                    out=dst, in_=outt[ob][:, :, :]), inc=[(sO[ob], 16)])
                outt_free[ob] = (sO[ob], sO[ob].n)

        for t in range(NT):
            emit_4c(t)
            if t % 4 == 3:
                emit_rstd4(t // 4)
            if t >= LAG4:
                emit_4d(t - LAG4)
        for t in range(NT - LAG4, NT):
            emit_4d(t)
        ln4_stats_V = sV.n
        ps_close("ph31", "ph30")

        if debug:
            dump("st4", stB_all[:, :, :], sV, ln4_stats_V)
            dump("fbf", f_bf[:, :, :], sV, ln4_stats_V)
        p.wait("gpsimd", sO0, sO0.n)
        p.wait("gpsimd", sO1, sO1.n)

    @blk.sync
    def _(sync):
        walk(P("sync", sync, sems))
        walk_state["first"] = False

    @blk.gpsimd
    def _(gpsimd):
        walk(P("gpsimd", gpsimd, sems))

    @blk.tensor
    def _(tensor):
        walk(P("tensor", tensor, sems))

    @blk.vector
    def _(vector):
        walk(P("vector", vector, sems))

    @blk.scalar
    def _(scalar):
        walk(P("scalar", scalar, sems))

    return nc


def kernel(L0_attn_x, L0_attn_w_qkv, L0_attn_w_proj, L0_attn_b_proj,
           L0_ffn_weight, L0_ffn_aux_weight1, L0_ffn_aux_weight2, L0_ffn_bias):
    nc = build_program()
    wqkv = np.ascontiguousarray(np.asarray(L0_attn_w_qkv, np.float32))
    wproj = np.ascontiguousarray(np.asarray(L0_attn_w_proj, np.float32))
    bproj = np.ascontiguousarray(np.asarray(L0_attn_b_proj, np.float32)).reshape(1, C)
    w1 = np.ascontiguousarray(np.asarray(L0_ffn_weight, np.float32))
    w2 = np.ascontiguousarray(np.asarray(L0_ffn_aux_weight1, np.float32))
    w3 = np.ascontiguousarray(np.asarray(L0_ffn_aux_weight2, np.float32))
    fb = np.ascontiguousarray(np.asarray(L0_ffn_bias, np.float32)).reshape(1, C)
    xs = np.asarray(L0_attn_x, np.float32)
    in_maps = []
    for b in range(8):
        in_maps.append(dict(x=np.ascontiguousarray(xs[b]), wqkv=wqkv, wproj=wproj,
                            bproj=bproj, w1=w1, w2=w2, w3=w3, fbias=fb))
    res = run_bass_kernel_spmd(nc, in_maps, core_ids=list(range(8)))
    return np.stack([np.asarray(res.results[b]["out"], np.float32) for b in range(8)], 0)
